# version 13
# baseline (speedup 1.0000x reference)
import numpy as np
import ml_dtypes
import concourse.bass as bass
import concourse.mybir as mybir
from concourse.bass_utils import run_bass_kernel_spmd
from contextlib import ExitStack

F32 = mybir.dt.float32
BF16 = mybir.dt.bfloat16
U8 = mybir.dt.uint8
I32 = mybir.dt.int32
AF = mybir.ActivationFunctionType
OP = mybir.AluOpType
AX = mybir.AxisListType.X

B, T, D = 16, 2048, 2048
H, DH = 16, 128
NF = 16384
NCORES = 8
BC = B // NCORES
KC = D // 128
TT = T // 128
TCH = T // 512
SOFT = 1.0 / float(np.sqrt(DH))
EPS = 1e-5
P126 = float(2.0 ** 126)

GU_CC = 8
NSTR = NF // 512
DN_CC = 64
HST = D // 512
NQ = 4
NCHQ = 32


class Sched:

    def __init__(self):
        self.ops = {"sync": [], "scalar": [], "vector": [], "tensor": [], "gpsimd": []}
        self.counts = {}
        self.phase = "init" 

    def count(self, sem):
        return self.counts.get(sem, 0)

    def emit(self, eng, fn, waits=(), inc=None, incn=1, drain=False, prenop=False,
             pad=False):
        ws = [(s, v) for (s, v) in waits if v > 0]
        self.ops[eng].append((fn, ws, inc, incn, drain, prenop, pad, self.phase))
        if inc is not None:
            self.counts[inc] = self.counts.get(inc, 0) + incn

    def dma(self, eng, fn, waits=(), inc=None):
        self.emit(eng, fn, waits, inc, 16)


LN_TILES = TT


def build_nc(phases=("ln", "att", "wo", "deq", "ffn"), debug=()):
    nc = bass.Bass("TRN2", target_bir_lowering=False)

    x_d = nc.dram_tensor("x", [BC, T, D], F32, kind="ExternalInput").ap()
    wqT = nc.dram_tensor("wqT", [D, D], F32, kind="ExternalInput").ap()
    wkT = nc.dram_tensor("wkT", [D, D], F32, kind="ExternalInput").ap()
    wvT = nc.dram_tensor("wvT", [D, D], F32, kind="ExternalInput").ap()
    woT = nc.dram_tensor("woT", [D, D], F32, kind="ExternalInput").ap()
    wgB = nc.dram_tensor("wgB", [1024, NF], U8, kind="ExternalInput").ap()
    sgE = nc.dram_tensor("sgE", [64, NF], F32, kind="ExternalInput").ap()
    wuB = nc.dram_tensor("wuB", [1024, NF], U8, kind="ExternalInput").ap()
    suE = nc.dram_tensor("suE", [64, NF], F32, kind="ExternalInput").ap()
    wdB = nc.dram_tensor("wdB", [NF // 2, D], U8, kind="ExternalInput").ap()
    sdE = nc.dram_tensor("sdE", [512, D], F32, kind="ExternalInput").ap()
    idf = nc.dram_tensor("idf", [128, 128], F32, kind="ExternalInput").ap()
    out_d = nc.dram_tensor("out", [BC, T, D], F32, kind="ExternalOutput").ap()
    dbg = {}
    for nm, shp, ddt in (("xnT", [128, KC, T], BF16), ("qT", [128, T], BF16),
                         ("kT", [128, T], BF16), ("vtok", [128, KC, 128], BF16),
                         ("rrow", [1, T], F32), ("negc", [128, 16], F32),
                         ("ptb", [128, KC, 512], BF16), ("att", [T, D], F32),
                         ("wg0", [KC, 128, 512], BF16), ("wd0", [128, D], BF16),
                         ("intq", [128, NCHQ, 512], BF16), ("ffa", [128, 4, D], F32),
                         ("lnst", [128, 12], F32), ("xnt", [128, 2, D], BF16),
                         ("lnT", [128, 16], F32), ("sdW", [128, 16], F32),
                         ("wgs", [128, 16, 256], BF16), ("gg", [128, 512], BF16),
                         ("wds", [128, 4, 512], BF16),
                         ("wg31", [KC, 128, 512], BF16),
                         ("wu31", [KC, 128, 512], BF16),
                         ("wd63", [256, D], BF16),
                         ("lnR", [128, 16], F32), ("lnS", [128, 16], F32),
                         ("lnQ", [128, 16], F32)):
        dbg[nm] = nc.dram_tensor("dbg_" + nm, shp, ddt, kind="ExternalOutput").ap()

    wgC = nc.dram_tensor("wgC", [NSTR, KC, 128, 512], BF16).ap()
    wuC = nc.dram_tensor("wuC", [NSTR, KC, 128, 512], BF16).ap()
    wdC = nc.dram_tensor("wdC", [NF, D], BF16).ap()
    lnScr = nc.dram_tensor("lnScr", [128, 64], F32).ap()
    aM4 = nc.dram_tensor("aM4", [128, 1], F32).ap()
    aNegc = nc.dram_tensor("aNegc", [128, 16], F32).ap()
    aC0 = nc.dram_tensor("aC0", [1, 1], F32).ap()
    aRQ = nc.dram_tensor("aRQ", [128, 16], F32).ap()
    aScr2h = nc.dram_tensor("aScr2h", [16, 128], BF16).ap()
    aScr2l = nc.dram_tensor("aScr2l", [16, 128], BF16).ap()
    otD = nc.dram_tensor("otD", [BC, D, T], BF16).ap()
    atD = nc.dram_tensor("atD", [BC, T, D], F32).ap()

    sch = Sched()
    with ExitStack() as st:
        ctx = st.enter_context
        sb = {}

        def S(name, shape, dt):
            sb[name] = ctx(nc.sbuf_tensor(name, shape, dt))
            return sb[name]

        xnT = S("xnT", [128, KC, T], BF16)
        identb = S("identb", [128, 128], BF16)
        identf = S("identf", [128, 128], F32)
        ones_c = S("ones_c", [128, 1], BF16)
        ones_r = S("ones_r", [1, 128], BF16)
        ones_f = S("ones_f", [128, 1], F32)
        lnst = S("lnst", [128, 12], F32)
        nrm = S("nrm", [128, 48], F32)
        rbuf = S("rbuf", [128, 64], F32)
        rQ = S("rQ", [128, 16], F32)
        rowsT = S("rowsT", [16, 256], F32)
        c2h16 = S("c2h16", [16, 128], BF16)
        c2l16 = S("c2l16", [16, 128], BF16)
        kapb = S("kapb", [1, 2], BF16)
        c0b = S("c0b", [128, 1], F32)
        rr16 = S("rr16", [1, 512], BF16)
        rrf = S("rrf", [1, 512], F32)
        rbc = S("rbc", [128, 512], F32)
        flA = S("flA", [128, 2], F32)
        negc = S("negc", [128, 16], F32)
        epst = S("epst", [128, 1], F32)
        lnS = S("lnS", [128, 16], F32)
        lnQ = S("lnQ", [128, 16], F32)
        lnM = S("lnM", [128, 16], F32)
        lnV = S("lnV", [128, 16], F32)
        lnT = S("lnT", [128, 16], F32)
        sdW = S("sdW", [128, 16], F32)
        lnR = S("lnR", [128, 16], F32)
        ones16 = S("ones16", [128, 16], F32)
        flT = S("flT", [128, 4], F32)
        pdump = S("pdump", [128, 512], BF16)
        otb = S("otb", [128, 2, 512], BF16)

        ARENA = 131072
        arena = ctx(nc.sbuf_tensor("arena", [128, ARENA], U8))
        abase = nc.lookup_mloc(arena).addr

        def AT(name, shape, dt, off):
            h = nc.alloc_sbuf_tensor_at(name, shape, dt, offset=abase + off)
            sb[name] = h
            return h

        xbuf = AT("xbuf", [128, 2, D], F32, 65536)
        xsq = AT("xsq", [128, D], F32, 81920)
        xnt_t = AT("xnt_t", [128, 2, D], BF16, 90112)
        wq_h = AT("wq_h", [128, 2, KC, 128], BF16, 0)
        wk_h = AT("wk_h", [128, 2, KC, 128], BF16, 8192)
        wv_h = AT("wv_h", [128, 2, KC, 128], BF16, 16384)
        qT = AT("qT", [128, T], BF16, 24576)
        kT = AT("kT", [128, T], BF16, 28672)
        vTt = AT("vTt", [128, T], BF16, 32768)
        vtok = AT("vtok", [128, KC, 128], BF16, 36864)
        sq2 = AT("sq2", [128, T], BF16, 40960)
        ptb = AT("ptb", [128, KC, 512], BF16, 45056)
        dqs = AT("dqs", [128, 2, 512], F32, 98304)
        dqi = AT("dqi", [128, 2, 512], I32, 102400)
        dqp = AT("dqp", [128, 512], I32, 106496)
        dqt2 = AT("dqt2", [128, 512], I32, 108544)
        dqu = AT("dqu", [128, 2, 512], I32, 110592)
        dqw = AT("dqw", [128, 8, 2, 512], BF16, 114688)
        rrow = AT("rrow", [1, T], F32, 80896)
        ncrow = AT("ncrow", [1, T], F32, 89088)
        crsc = AT("crsc", [1, T], F32, 97280)
        c2hi = AT("c2hi", [1, T], BF16, 105472)
        c2lo = AT("c2lo", [1, T], BF16, 109568)
        woS = AT("woS", [128, KC, D], BF16, 0)
        otl = AT("otl", [128, KC * 128], BF16, 65536)
        att_s = AT("att_s", [128, 2, D], F32, 69632)
        wgs = AT("wgs", [128, 2, KC, 256], BF16, 0)
        wus = AT("wus", [128, 2, KC, 256], BF16, 16384)
        wds = AT("wds", [128, 4, 512], BF16, 32768)
        gg = AT("gg", [128, 2, 512], BF16, 36864)
        intq = AT("intq", [128, NCHQ, 512], BF16, 40960)
        ffa = AT("ffa", [128, 4, D], F32, 73728)
        xfb = AT("xfb", [128, D], F32, 106496)
        afb = AT("afb", [128, D], F32, 114688)
        ob = AT("ob", [128, D], F32, 122880)

        ps = {}
        for nm in "ABCDEF":
            ps[nm] = ctx(nc.psum_tensor("ps" + nm, [128, 512], F32))
        psTR = ctx(nc.psum_tensor("psTR", [128, 1024], BF16))
        psR = ctx(nc.psum_tensor("psR", [128, 512], F32))

        sem_names = ["qS", "qW", "qA", "qG", "sT", "sV", "sA", "dqw16", "oW"]
        sems = {n: ctx(nc.semaphore(n)) for n in sem_names}

        E = sch.emit
        DMA = sch.dma
        C = sch.count

        def TGUARD(waits=()):
            E("tensor", lambda t: t.transpose(psR[0:1, 384:512], epst[:],
                                              identf[:]), waits=waits)

        def AFLUSH(waits=()):
            E("scalar", lambda sc: sc.activation(
                out=flA[:, 0:1], in_=epst[:], func=AF.Exp,
                accum_out=flA[:, 1:2]), waits=waits)

        def VFLUSH(waits=()):
            E("vector", lambda v: v.tensor_reduce(
                out=flT[:, 0:1], in_=ones16[:, 0:4], axis=AX, op=OP.add),
                waits=waits)
        state = {}

        DMA("gpsimd", lambda g: g.dma_start(out=identf[:], in_=idf), inc="qG")
        E("vector", lambda v: v.tensor_copy(identb[:], identf[:]),
          waits=[("qG", 16)], inc="sV", drain=True)
        E("vector", lambda v: v.memset(ones_c[:], 1.0), inc="sV", drain=True)
        E("vector", lambda v: v.memset(ones_r[:], 1.0), inc="sV", drain=True)
        E("vector", lambda v: v.memset(ones_f[:], 1.0), inc="sV", drain=True)
        E("vector", lambda v: v.memset(epst[:], EPS), inc="sV", drain=True)
        E("vector", lambda v: v.memset(ones16[:], 1.0), inc="sV", drain=True)

        def phase_ln(b):
            simple = "lnsimple" in phases
            for i in range(LN_TILES):
                sl = i % 2
                DMA("scalar", (lambda b=b, i=i, sl=sl: lambda e: e.dma_start(
                    out=xbuf[:, sl, :], in_=x_d[b, i * 128:(i + 1) * 128, :]))(),
                    waits=[("sV", C("sV")), ("sA", C("sA")), ("sT", C("sT")),
                           ("oW", C("oW")), ("qS", C("qS"))], inc="qA")
                wl = [("qA", C("qA"))]
                E("vector", (lambda i=i, sl=sl: lambda v: v.tensor_scalar(
                    out=xsq[:], in0=xbuf[:, sl, :], scalar1=1.0, scalar2=0.0,
                    op0=OP.mult, op1=OP.add, accum_out=lnS[:, i:i + 1]))(),
                    waits=wl)
                E("vector", (lambda i=i, sl=sl: lambda v: v.scalar_tensor_tensor(
                    out=xsq[:], in0=xbuf[:, sl, :], scalar=1.0,
                    in1=xbuf[:, sl, :], op0=OP.mult, op1=OP.mult,
                    accum_out=lnQ[:, i:i + 1]))())
                VFLUSH()
                E("vector", lambda v: v.engine_nop(), inc="sV", drain=True)
            acc_done = C("sV")
            DMA("gpsimd", lambda g: g.dma_start(out=lnScr[:, 0:16], in_=lnS[:]),
                waits=[("sV", acc_done)], inc="qG")
            DMA("gpsimd", lambda g: g.dma_start(out=lnScr[:, 16:32], in_=lnQ[:]),
                inc="qG")
            rt1o = C("qG")
            DMA("gpsimd", lambda g: g.dma_start(out=lnS[:], in_=lnScr[:, 0:16]),
                waits=[("qG", rt1o)], inc="qG")
            DMA("gpsimd", lambda g: g.dma_start(out=lnQ[:], in_=lnScr[:, 16:32]),
                inc="qG")
            rt1 = C("qG")
            E("vector", lambda v: v.tensor_scalar(
                out=lnM[:], in0=lnS[:], scalar1=-1.0 / D, scalar2=None,
                op0=OP.mult), waits=[("qG", rt1)], pad=True)
            E("vector", lambda v: v.tensor_scalar(
                out=lnV[:], in0=lnQ[:], scalar1=1.0 / D, scalar2=None, op0=OP.mult),
                pad=True)
            E("vector", lambda v: v.tensor_tensor(
                out=sdW[:], in0=lnM[:], in1=lnM[:], op=OP.mult), pad=True)
            E("vector", lambda v: v.tensor_tensor(
                out=lnT[:], in0=lnV[:], in1=sdW[:], op=OP.subtract),
                inc="sV", drain=True)
            DMA("gpsimd", lambda g: g.dma_start(out=lnScr[:, 32:48], in_=lnT[:]),
                waits=[("sV", C("sV"))], inc="qG")
            DMA("gpsimd", lambda g: g.dma_start(out=lnT[:], in_=lnScr[:, 32:48]),
                waits=[("qG", C("qG"))], inc="qG")
            rt2 = C("qG")
            E("scalar", lambda sc: sc.activation(
                out=sdW[:], in_=lnT[:], func=AF.Sqrt, bias=epst[:]),
                waits=[("qG", rt2)], inc="sA", drain=True)
            DMA("gpsimd", lambda g: g.dma_start(out=lnScr[:, 48:64], in_=sdW[:]),
                waits=[("sA", C("sA"))], inc="qG")
            DMA("gpsimd", lambda g: g.dma_start(out=sdW[:], in_=lnScr[:, 48:64]),
                waits=[("qG", C("qG"))], inc="qG")
            rt3 = C("qG")
            E("vector", lambda v: v.reciprocal(lnR[:], sdW[:]),
                waits=[("qG", rt3)], inc="sV", drain=True)
            loads = []
            for i in range(LN_TILES):
                sl = i % 2
                wl3 = [("sV", C("sV")), ("qS", C("qS"))]
                DMA("scalar", (lambda b=b, i=i, sl=sl: lambda e: e.dma_start(
                    out=xbuf[:, sl, :], in_=x_d[b, i * 128:(i + 1) * 128, :]))(),
                    waits=wl3, inc="qA")
                ld = C("qA")
                pview = xnt_t[:, sl, :].rearrange("p (two m) -> p m two", two=2)
                if simple:
                    E("vector", (lambda sl=sl, pview=pview: lambda v: v.tensor_scalar(
                        out=pview, in0=xbuf[:, sl, :], scalar1=1.0,
                        scalar2=None, op0=OP.mult))(),
                        waits=[("qA", ld)], inc="sV", drain=True)
                else:
                    E("vector", (lambda i=i, sl=sl, pview=pview: lambda v:
                        v.scalar_tensor_tensor(
                            out=pview, in0=xbuf[:, sl, :],
                            scalar=lnM[:, i:i + 1],
                            in1=lnR[:, i:i + 1].broadcast_to([128, D]),
                            op0=OP.add, op1=OP.mult))(),
                        waits=[("qA", ld)], inc="sV", drain=True)
                stt_done = C("sV")
                for c in range(KC):
                    DMA("sync", (lambda sl=sl, c=c, i=i: lambda e: e.dma_start(
                        out=xnT[:, c, i * 128:(i + 1) * 128],
                        in_=xnt_t[:, sl, c * 128:(c + 1) * 128],
                        transpose=True))(),
                        waits=[("sV", stt_done)], inc="qS")

        def load_head_w(b, h, sl):
            bar = [("sT", C("sT")), ("sV", C("sV")), ("sA", C("sA")),
                   ("oW", C("oW"))]
            for (w_d, dst) in ((wqT, wq_h), (wkT, wk_h), (wvT, wv_h)):
                for g4 in range(4):
                    DMA("gpsimd", (lambda w_d=w_d, dst=dst, sl=sl, h=h, g4=g4:
                        lambda g: g.dma_start(
                            out=dst[:, sl, g4 * 4:(g4 + 1) * 4, :],
                            in_=w_d.rearrange("(c p) d -> p c d", p=128)[
                                :, g4 * 4:(g4 + 1) * 4,
                                h * 128:(h + 1) * 128]))(),
                        waits=bar, inc="qG")

        def phase_qkv(b, h, sl, projs):
            qg_done = C("qG")
            for (wbuf, obuf) in projs:
                for tc in range(TCH):
                    bank = "E" if tc % 2 == 0 else "F"
                    for c in range(KC):
                        wt = [("qG", qg_done)]
                        if c == 0:
                            wt += [("sV", C("sV")), ("sA", C("sA"))]
                        E("tensor", (lambda wbuf=wbuf, sl=sl, c=c, tc=tc, bank=bank:
                            lambda t: t.matmul(
                                ps[bank][:], wbuf[:, sl, c, :],
                                xnT[:, c, tc * 512:(tc + 1) * 512],
                                start=(c == 0), stop=(c == KC - 1)))(),
                            waits=wt, inc="sT")
                    E("vector", (lambda obuf=obuf, tc=tc, bank=bank: lambda v:
                        v.tensor_copy(obuf[:, tc * 512:(tc + 1) * 512], ps[bank][:]))(),
                        waits=[("sT", C("sT")), ("sA", C("sA"))], inc="sV",
                        drain=True)
                    if (b == 0 and state.get("dq_feed") and "qkv" in FEEDMASK
                            and tc == 3):
                        state["dq_feed"](1)

        def phase_vtok(b, h):
            vt_done = C("sV")
            for c in range(KC):
                slot = c % 8
                E("tensor", (lambda c=c, slot=slot: lambda t: t.transpose(
                    psTR[:, slot * 128:(slot + 1) * 128],
                    vTt[:, c * 128:(c + 1) * 128], identb[:]))(),
                    waits=[("sV", vt_done if c < 8 else C("sV") - 7)], inc="sT")
                E("vector", (lambda c=c, slot=slot: lambda v: v.tensor_copy(
                    vtok[:, c, :], psTR[:, slot * 128:(slot + 1) * 128]))(),
                    waits=[("sT", C("sT"))], inc="sV", prenop=True)

        def phase_norms(b, h):
            qk_done = C("sV")
            E("scalar", lambda sc: sc.activation(out=sq2[:], in_=kT[:],
                                                 func=AF.Square),
              waits=[("sV", qk_done)], inc="sA", drain=True)
            TGUARD(waits=[("sA", C("sA"))])
            for j in range(TT):
                E("tensor", (lambda j=j: lambda t: t.matmul(
                    psR[:, j:j + 1], sq2[:, j * 128:(j + 1) * 128], ones_c[:],
                    start=True, stop=True))(), inc="sT")
            if b == 0 and state.get("dq_feed") and "norms" in FEEDMASK:
                state["dq_feed"](2)
            E("vector", lambda v: v.tensor_copy(nrm[:, 0:16], psR[:, 0:16]),
              waits=[("sT", C("sT"))], prenop=True, pad=True)
            E("vector", lambda v: v.tensor_scalar(
                out=rbuf[:, 0:16], in0=nrm[:, 0:16], scalar1=1.0, scalar2=-3.0e38,
                op0=OP.mult, op1=OP.max, accum_out=nrm[:, 16:17]))
            VFLUSH()
            E("vector", lambda v: v.tensor_tensor(
                out=nrm[:, 17:18], in0=nrm[:, 16:17], in1=nrm[:, 16:17],
                op=OP.mult), pad=True)
            E("vector", lambda v: v.tensor_tensor(
                out=nrm[:, 18:19], in0=nrm[:, 17:18], in1=nrm[:, 17:18],
                op=OP.mult), inc="sV", drain=True)
            DMA("gpsimd", lambda g: g.dma_start(out=aM4,
                                                in_=nrm[:, 18:19]),
                waits=[("sV", C("sV"))], inc="qG")
            DMA("gpsimd", lambda g: g.dma_start(out=nrm[:, 18:19],
                                                in_=aM4),
                waits=[("qG", C("qG"))], inc="qG")
            E("tensor", lambda t: t.matmul(psR[0:1, 20:21], nrm[:, 18:19],
                                           ones_f[:], start=True, stop=True),
              waits=[("qG", C("qG"))], inc="sT")
            E("scalar", lambda sc: sc.activation(out=flA[0:1, 0:1],
                                                 in_=psR[0:1, 20:21],
                                                 func=AF.Sqrt),
              waits=[("sT", C("sT"))], prenop=True, pad=True)
            E("scalar", lambda sc: sc.activation(out=flA[0:1, 0:1],
                                                 in_=flA[0:1, 0:1], func=AF.Sqrt),
              pad=True)
            E("scalar", lambda sc: sc.activation(out=kapb[0:1, 0:1],
                                                 in_=flA[0:1, 0:1], func=AF.Sqrt),
              inc="sA", drain=True)
            E("tensor", lambda t: t.matmul(psR[:, 22:23], ones_r[:],
                                           kapb[0:1, 0:1], start=True, stop=True),
              waits=[("sA", C("sA"))], inc="sT")
            E("vector", lambda v: v.tensor_copy(nrm[:, 19:20], psR[:, 22:23]),
              waits=[("sT", C("sT"))], inc="sV", prenop=True, pad=True, drain=True)
            if b == 0 and state.get("dq_feed") and "norms" in FEEDMASK:
                state["dq_feed"](2)
            E("scalar", lambda sc: sc.activation(out=sq2[:], in_=qT[:],
                                                 func=AF.Square),
              waits=[("sV", C("sV"))], inc="sA", drain=True)
            TGUARD(waits=[("sA", C("sA"))])
            for j in range(TT):
                E("tensor", (lambda j=j: lambda t: t.matmul(
                    psR[:, 24 + j:25 + j], sq2[:, j * 128:(j + 1) * 128],
                    ones_c[:], start=True, stop=True))(), inc="sT")
            E("vector", lambda v: v.tensor_copy(nrm[:, 24:40], psR[:, 24:40]),
              waits=[("sT", C("sT"))], inc="sV", prenop=True, pad=True, drain=True)
            E("scalar", lambda sc: sc.activation(out=nrm[:, 24:40],
                                                 in_=nrm[:, 24:40], func=AF.Sqrt),
              waits=[("sV", C("sV"))], inc="sA", drain=True)
            E("vector", lambda v: v.scalar_tensor_tensor(
                out=negc[:, 0:16], in0=nrm[:, 24:40], scalar=-SOFT,
                in1=nrm[:, 19:20].broadcast_to([128, 16]), op0=OP.mult,
                op1=OP.mult),
              waits=[("sA", C("sA"))], inc="sV", drain=True)
            DMA("gpsimd", lambda g: g.dma_start(out=aNegc,
                                                in_=negc[:, 0:16]),
                waits=[("sV", C("sV"))], inc="qG")
            DMA("gpsimd", lambda g: g.dma_start(out=negc[:, 0:16],
                                                in_=aNegc),
                waits=[("qG", C("qG"))], inc="qG")
            state["negc_rt"] = C("qG")
            E("vector", lambda v: v.tensor_scalar(
                out=rbuf[:, 0:16], in0=negc[:, 0:16], scalar1=1.0,
                scalar2=3.0e38, op0=OP.mult, op1=OP.min,
                accum_out=nrm[:, 41:42]),
              waits=[("qG", state["negc_rt"])])
            VFLUSH()
            E("vector", lambda v: v.engine_nop(), inc="sV", drain=True)
            DMA("gpsimd", lambda g: g.dma_start(out=aM4, in_=nrm[:, 41:42]),
                waits=[("sV", C("sV"))], inc="qG")
            DMA("gpsimd", lambda g: g.dma_start(
                out=rowsT[0:1, 0:128],
                in_=aM4.rearrange("a b -> (a b)").unsqueeze(0)),
                waits=[("qG", C("qG"))], inc="qG")
            E("vector", lambda v: v.tensor_scalar(
                out=rowsT[0:1, 128:256], in0=rowsT[0:1, 0:128], scalar1=1.0,
                scalar2=3.0e38, op0=OP.mult, op1=OP.min,
                accum_out=rowsT[0:1, 200:201]),
              waits=[("qG", C("qG"))], pad=True)
            VFLUSH()
            E("vector", lambda v: v.engine_nop(), inc="sV", drain=True)
            DMA("gpsimd", lambda g: g.dma_start(out=aC0, in_=rowsT[0:1, 200:201]),
                waits=[("sV", C("sV"))], inc="qG")
            DMA("gpsimd", lambda g: g.dma_start(
                out=c0b[:], in_=aC0.broadcast_to([128, 1])),
                waits=[("qG", C("qG"))], inc="qG")
            state["c0_rt"] = C("qG")
            if b == 0 and state.get("dq_feed") and "norms" in FEEDMASK:
                state["dq_feed"](3)

        def phase_pass1(b, h):
            pass

        def phase_crow(b, h):
            pass

        def phase_pass2_pv(b, h):
            c0_rt = state["c0_rt"]
            for qc in range(TCH):
                for kt in range(KC):
                    bank = "A" if kt % 2 == 0 else "B"
                    wt = [("qG", c0_rt)]
                    if kt >= 2 or qc > 0:
                        wt.append(("sA", C("sA") - 1 if kt >= 2 else C("sA")))
                    E("tensor", (lambda kt=kt, qc=qc, bank=bank: lambda t: t.matmul(
                        ps[bank][:], kT[:, kt * 128:(kt + 1) * 128],
                        qT[:, qc * 512:(qc + 1) * 512],
                        start=True, stop=True))(), waits=wt, inc="sT")
                    E("scalar", (lambda kt=kt, bank=bank: lambda s: s.activation(
                        out=ptb[:, kt, :], in_=ps[bank][:], func=AF.Exp,
                        bias=c0b[:, 0:1], scale=SOFT))(),
                        waits=[("sT", C("sT"))], inc="sA", drain=(kt == KC - 1))
                if b == 0 and state.get("dq_feed") and "pass2" in FEEDMASK:
                    state["dq_feed"](3)
                pt_done = C("sA")
                for kt in range(KC):
                    wt = [("sA", pt_done)]
                    if kt == 0:
                        wt.append(("sV", C("sV")))
                    E("tensor", (lambda kt=kt: lambda t: t.matmul(
                        ps["C"][:], vtok[:, kt, :], ptb[:, kt, :],
                        start=(kt == 0), stop=(kt == KC - 1)))(),
                        waits=wt, inc="sT")
                for kt in range(KC):
                    wt = [("sA", pt_done)]
                    if kt == 0:
                        wt.append(("sV", C("sV")))
                    E("tensor", (lambda kt=kt: lambda t: t.matmul(
                        ps["D"][0:1, :], ones_c[:, 0:1], ptb[:, kt, :],
                        start=(kt == 0), stop=(kt == KC - 1)))(),
                        waits=wt, inc="sT")
                E("vector", lambda v: v.reciprocal(rrf[0:1, :], ps["D"][0:1, :]),
                  waits=[("sT", C("sT"))], prenop=True, pad=True)
                E("vector", lambda v: v.tensor_copy(rr16[0:1, :], rrf[0:1, :]),
                  inc="sV", pad=True, drain=True)
                E("tensor", lambda t: t.matmul(ps["E"][:], ones_r[:],
                                               rr16[0:1, :],
                                               start=True, stop=True),
                  waits=[("sV", C("sV"))], inc="sT")
                E("vector", lambda v: v.tensor_copy(rbc[:], ps["E"][:]),
                  waits=[("sT", C("sT"))])
                osl = qc % 2
                E("vector", (lambda osl=osl: lambda v: v.tensor_tensor(
                    out=otb[:, osl, :], in0=ps["C"][:], in1=rbc[:],
                    op=OP.mult))(),
                    waits=[("qG", C("qG"))], inc="sV",
                    drain=True)
                DMA("gpsimd", (lambda b=b, h=h, qc=qc, osl=osl: lambda g: g.dma_start(
                    out=otD[b, h * 128:(h + 1) * 128, qc * 512:(qc + 1) * 512],
                    in_=otb[:, osl, :]))(),
                    waits=[("sV", C("sV"))], inc="qG")

        def phase_wo(b):
            for grp in range(4):
                DMA("gpsimd", (lambda grp=grp: lambda g: g.dma_start(
                    out=woS[:, grp * 4:(grp + 1) * 4, :],
                    in_=woT.rearrange("(c p) d -> p c d", p=128)[
                        :, grp * 4:(grp + 1) * 4, :]))(),
                    waits=[("sT", C("sT")), ("sA", C("sA")), ("sV", C("sV"))],
                    inc="qG")
            wo_done = C("qG")
            for tt in range(TT):
                DMA("scalar", (lambda b=b, tt=tt: lambda e: e.dma_start(
                    out=otl[:].rearrange("p (c q) -> p c q", c=KC),
                    in_=otD[b].rearrange("(c p) t -> p c t", p=128)[
                        :, :, tt * 128:(tt + 1) * 128]))(),
                    waits=[("sT", C("sT")), ("qG", wo_done)], inc="qA")
                DMA("scalar", lambda e: e.dma_start(
                    out=lnScr[0:1, 32:40], in_=lnScr[0:1, 40:48]), inc="qA")
                ld = C("qA")
                TGUARD(waits=[("qA", ld)])
                banks = ["A", "B", "C", "D"]
                for c in range(KC):
                    for dsl in range(4):
                        wt = [("qA", ld)]
                        if c == 0:
                            wt.append(("sV", C("sV")))
                        E("tensor", (lambda c=c, dsl=dsl, bank=banks[dsl]:
                            lambda t: t.matmul(
                                ps[bank][:],
                                otl[:, c * 128:(c + 1) * 128],
                                woS[:, c, dsl * 512:(dsl + 1) * 512],
                                start=(c == 0), stop=(c == KC - 1)))(),
                            waits=wt, inc="sT")
                t_done = C("sT")
                esl = tt % 2
                for dsl in range(4):
                    E("vector", (lambda dsl=dsl, esl=esl, bank=banks[dsl]: lambda v:
                        v.tensor_copy(att_s[:, esl, dsl * 512:(dsl + 1) * 512],
                                      ps[bank][:]))(),
                        waits=[("sT", t_done), ("qG", C("qG"))], inc="sV",
                        drain=(dsl == 3))
                DMA("gpsimd", (lambda b=b, tt=tt, esl=esl: lambda g: g.dma_start(
                    out=atD[b, tt * 128:(tt + 1) * 128, :], in_=att_s[:, esl, :]))(),
                    waits=[("sV", C("sV"))], inc="qG")
                if b == 0 and state.get("dq_feed") and "wo" in FEEDMASK:
                    state["dq_feed"](2)
            state["wo_stores_%d" % b] = C("qG")

        def dequant_tile(w_in, s_in, cch, n0, store_fns, sl):
            w = 512
        DQW_SLOTS = 8
        DRB = 6

        def dq_loads(tile, ti, dq_state):
            (w_in, s_in, cch, n0, _sf) = tile
            w = 512
            s2 = ti % 2
            sv_after = dq_state["sv_after"]
            wl = []
            if ti < 2:
                wl += dq_state.get("bar", [])
            if ti >= 2 and (ti - 2) in sv_after:
                wl.append(("sV", sv_after[ti - 2]))
            DMA("gpsimd", (lambda w_in=w_in, cch=cch, n0=n0, s2=s2: lambda e:
                e.dma_start(out=dqi[:, s2, :],
                            in_=w_in[cch * 128:(cch + 1) * 128, n0:n0 + w]))(),
                waits=wl, inc="qG")
            DMA("scalar", (lambda s_in=s_in, cch=cch, n0=n0, s2=s2: lambda e:
                e.dma_start(out=dqs[:, s2, :],
                            in_=s_in[cch * 8:(cch + 1) * 8, n0:n0 + w]
                            .unsqueeze(1).broadcast_to([8, 16, w])))(),
                waits=wl, inc="qA")
            return (C("qG"), C("qA"))

        def dequant_tile(store_fns, ti, succ, dq_state):
            s2 = ti % 2
            sl = ti % DQW_SLOTS
            sv_after = dq_state["sv_after"]
            st_cum = dq_state["st_cum"]
            wl2 = [("qG", succ[0]), ("qA", succ[1])]
            if ti < 2:
                wl2 += dq_state.get("bar", [])
            if ti >= DQW_SLOTS:
                wl2.append(("dqw16", st_cum[ti - DQW_SLOTS]))
            E("vector", (lambda s2=s2: lambda v: v.tensor_scalar(
                out=dqs[:, s2, :], in0=dqs[:, s2, :], scalar1=P126,
                scalar2=None, op0=OP.mult))(), waits=wl2)
            for lh, (m1, sh1, m2, sh2) in enumerate(((8, 12, 7, 6),
                                                     (128, 8, 112, 2))):
                E("vector", (lambda s2=s2, m1=m1, sh1=sh1: lambda v: v.tensor_scalar(
                    out=dqt2[:], in0=dqi[:, s2, :], scalar1=m1, scalar2=sh1,
                    op0=OP.bitwise_and, op1=OP.logical_shift_left))())
                E("vector", (lambda s2=s2, m2=m2, sh2=sh2: lambda v: v.tensor_scalar(
                    out=dqp[:], in0=dqi[:, s2, :], scalar1=m2, scalar2=sh2,
                    op0=OP.bitwise_and, op1=OP.logical_shift_left))())
                E("vector", (lambda lh=lh: lambda v: v.tensor_tensor(
                    out=dqu[:, lh, :], in0=dqt2[:], in1=dqp[:],
                    op=OP.bitwise_or))())
            for lh in range(2):
                E("vector", (lambda lh=lh, sl=sl, s2=s2: lambda v: v.tensor_tensor(
                    out=dqw[:, sl, lh, :],
                    in0=dqu[:, lh, :].bitcast(BF16)[:, 0:2 * 512:2],
                    in1=dqs[:, s2, :], op=OP.mult))(), inc="sV")
            sv_after[ti] = C("sV")
            dq_state["pend"].append((store_fns, sl, ti))

        def dq_flush(dq_state):
            if not dq_state["pend"]:
                return
            E("vector", lambda v: v.engine_nop(), inc="sV", drain=True)
            ms = C("sV")
            for (sfns, sl, ti) in dq_state["pend"]:
                for lh in range(2):
                    DMA("sync", (lambda fn=sfns[lh], sl=sl, lh=lh: lambda e:
                        fn(e, dqw[:, sl, lh, :]))(),
                        waits=[("sV", ms)], inc="dqw16")
                dq_state["st_cum"][ti] = C("dqw16")
            dq_state["pend"] = []

        def dq_tile_list():
            tiles = []
            for (w_in, s_in, cache) in ((wgB, sgE, wgC), (wuB, suE, wuC)):
                for strip in range(NSTR):
                    for cch in range(GU_CC):
                        sf = []
                        for lh in range(2):
                            kc = cch + 8 * lh

                            def mk(cache=cache, strip=strip, kc=kc):
                                def go(e, src):
                                    return e.dma_start(out=cache[strip, kc], in_=src)
                                return go
                            sf.append(mk())
                        tiles.append((w_in, s_in, cch, strip * 512, sf))
            for cch in range(DN_CC):
                for hs in range(HST):
                    sf = []
                    for lh in range(2):
                        r0 = (cch + 64 * lh) * 128

                        def mk(r0=r0, hs=hs):
                            def go(e, src):
                                return e.dma_start(
                                    out=wdC[r0:r0 + 128, hs * 512:(hs + 1) * 512],
                                    in_=src)
                            return go
                        sf.append(mk())
                    tiles.append((wdB, sdE, cch, hs * 512, sf))
            return tiles

        def dq_make_feeder():
            dq_state = {"pend": [], "sv_after": {}, "st_cum": {}}
            tiles = dq_tile_list()
            n = len(tiles)
            succ = {}
            cur = {"i": 0, "primed": False}

            def prime():
                if cur["primed"]:
                    return
                cur["primed"] = True
                dq_state["bar"] = [("qS", C("qS"))]
                succ[0] = dq_loads(tiles[0], 0, dq_state)
                succ[1] = dq_loads(tiles[1], 1, dq_state)

            def feed(k):
                prime()
                for _ in range(k):
                    ti = cur["i"]
                    if ti >= n:
                        return
                    dequant_tile(tiles[ti][4], ti, succ[ti + 1], dq_state)
                    if ti + 2 < n:
                        succ[ti + 2] = dq_loads(tiles[ti + 2], ti + 2, dq_state)
                    elif ti + 2 == n:
                        DMA("gpsimd", lambda e: e.dma_start(
                            out=lnScr[0:1, 16:24], in_=lnScr[0:1, 24:32]),
                            inc="qG")
                        DMA("scalar", lambda e: e.dma_start(
                            out=lnScr[0:1, 0:8], in_=lnScr[0:1, 8:16]),
                            inc="qA")
                        succ[n] = (C("qG"), C("qA"))
                    if (ti + 1) % DRB == 0:
                        dq_flush(dq_state)
                    cur["i"] = ti + 1

            def finish():
                while cur["i"] < n:
                    feed(DRB)
                dq_flush(dq_state)
            return feed, finish

        def phase_ffn(b):
            state.setdefault("gcnt", 0)
            state.setdefault("gsA", {})
            state.setdefault("gsV", {})
            dq_total = C("dqw16")
            wo_stores = state["wo_stores_%d" % b]
            gu_free = {0: C("sT"), 1: C("sT")}
            wd_free = [C("sT")] * 4
            TCr = 1 if "q0" in debug else TCH
            NQr = 1 if "q0" in debug else NQ
            def g_load(tc, q, s2):
                strip = q * 8 + s2 // 2
                half = s2 % 2
                gsl = s2 % 2
                for (cache, dst) in ((wgC, wgs), (wuC, wus)):
                    DMA("sync", (lambda cache=cache, dst=dst, strip=strip,
                                 half=half, gsl=gsl: lambda e: e.dma_start(
                        out=dst[:, gsl, :, :],
                        in_=cache[strip].rearrange("c p n -> p c n")[
                            :, :, half * 256:(half + 1) * 256]))(),
                        waits=[("dqw16", dq_total), ("sT", gu_free[gsl])],
                        inc="qW")
                return C("qW")

            pre_ld = {}
            for tc in range(TCr):
                for q in range(NQr):
                    dEv = C("sV")
                    for s2 in range(16):
                        gsl = s2 % 2
                        if (tc, q, s2) in pre_ld:
                            ld = pre_ld.pop((tc, q, s2))
                        else:
                            ld = g_load(tc, q, s2)
                        TGUARD(waits=[("qW", ld)])
                        for nsl in range(2):
                            nq_i = s2 * 2 + nsl
                            par = state["gcnt"] % 2
                            gb, ub = ("E", "F") if par == 0 else ("A", "B")
                            g2 = state["gcnt"] - 2
                            for (wbuf, bank) in ((wgs, gb), (wus, ub)):
                                for c in range(KC):
                                    wt = [("qW", ld)]
                                    if c == 0:
                                        if g2 in state["gsA"]:
                                            wt += [("sA", state["gsA"][g2]),
                                                   ("sV", max(state["gsV"][g2],
                                                              dEv))]
                                        else:
                                            wt += [("sA", C("sA")),
                                                   ("sV", C("sV"))]
                                    E("tensor", (lambda wbuf=wbuf, gsl=gsl, c=c,
                                                 nsl=nsl, tc=tc, bank=bank:
                                        lambda t: t.matmul(
                                            ps[bank][:],
                                            wbuf[:, gsl, c,
                                                 nsl * 128:(nsl + 1) * 128],
                                            xnT[:, c, tc * 512:(tc + 1) * 512],
                                            start=(c == 0), stop=(c == KC - 1)))(),
                                        waits=wt, inc="sT")
                            mmd = C("sT")
                            wg = [("sT", mmd)]
                            wg.append(("sV", state["gsV"][g2]) if g2 in
                                      state["gsV"] else ("sV", C("sV")))
                            E("scalar", (lambda par=par, gb=gb: lambda s:
                                s.activation(out=gg[:, par, :], in_=ps[gb][:],
                                             func=AF.Gelu_apprx_tanh))(),
                                waits=wg, inc="sA", drain=True)
                            state["gsA"][state["gcnt"]] = C("sA")
                            E("vector", (lambda nq_i=nq_i, par=par, ub=ub:
                                lambda v: v.tensor_tensor(
                                    out=intq[:, nq_i, :], in0=ps[ub][:],
                                    in1=gg[:, par, :], op=OP.mult))(),
                                waits=[("sA", state["gsA"][state["gcnt"]]),
                                       ("sT", mmd)], inc="sV",
                                drain=(nq_i == 31))
                            state["gsV"][state["gcnt"]] = C("sV")
                            state["gcnt"] += 1
                        gu_free[gsl] = C("sT")
                        if ("ffn0" in debug and b == 0 and tc == 0 and q == 0
                                and s2 == 0):
                            dump("wgs", wgs[:, 0, :, :])
                            dump("gg", gg[:, 0, :])
                    g_done = C("sV")
                    if "ffn0" in debug and b == 0 and tc == 0 and q == 0:
                        dump("intq", intq[:])
                    ids = [(hp, ch) for hp in range(HST) for ch in range(NCHQ)]

                    def issue_wds(j):
                        hp, ch = ids[j]
                        npr = q * NCHQ + ch
                        dsl = j % 4
                        DMA("sync", (lambda npr=npr, hp=hp, dsl=dsl: lambda e:
                            e.dma_start(
                                out=wds[:, dsl, :],
                                in_=wdC[npr * 128:(npr + 1) * 128,
                                        hp * 512:(hp + 1) * 512]))(),
                            waits=[("sT", wd_free[dsl])], inc="qS")
                        return C("qS")

                    nids = len(ids)
                    ld_after = [None] * (nids + 2)
                    ld_after[0] = issue_wds(0)
                    ld_after[1] = issue_wds(1)
                    for j, (hp, ch) in enumerate(ids):
                        if j + 2 < nids:
                            ld_after[j + 2] = issue_wds(j + 2)
                        else:
                            DMA("sync", lambda e: e.dma_start(
                                out=lnScr[0:1, 0:8], in_=lnScr[0:1, 8:16]),
                                inc="qS")
                            ld_after[j + 2] = C("qS")
                        dsl = j % 4
                        for tt4 in range(4):
                            wt = [("qS", ld_after[j + 1 if tt4 > 0 else j + 2]),
                                  ("sV", g_done)]
                            if ch == 0:
                                wt += [("sV", C("sV")), ("sA", C("sA"))]
                            E("tensor", (lambda ch=ch, tt4=tt4, dsl=dsl,
                                         bank="ABCD"[tt4]: lambda t: t.matmul(
                                ps[bank][:],
                                intq[:, ch, tt4 * 128:(tt4 + 1) * 128],
                                wds[:, dsl, :],
                                start=(ch == 0), stop=(ch == NCHQ - 1)))(),
                                waits=wt, inc="sT")
                        wd_free[dsl] = C("sT")
                        if ch == NCHQ - 1:
                            dn = C("sT")
                            for tt4 in range(4):
                                bank = "ABCD"[tt4]
                                if q == 0:
                                    E("vector", (lambda tt4=tt4, hp=hp, bank=bank:
                                        lambda v: v.tensor_copy(
                                            ffa[:, tt4, hp * 512:(hp + 1) * 512],
                                            ps[bank][:]))(),
                                        waits=[("sT", dn - 3 + tt4),
                                               ("qG", wo_stores)],
                                        inc="sV")
                                else:
                                    E("vector", (lambda tt4=tt4, hp=hp, bank=bank:
                                        lambda v: v.tensor_tensor(
                                            out=ffa[:, tt4, hp * 512:(hp + 1) * 512],
                                            in0=ps[bank][:],
                                            in1=ffa[:, tt4, hp * 512:(hp + 1) * 512],
                                            op=OP.add))(),
                                        waits=[("sT", dn - 3 + tt4)], inc="sV")
                for tt4 in range(4 if "q0" not in debug else 0):
                    tt = tc * 4 + tt4
                    DMA("gpsimd", (lambda b=b, tt=tt: lambda g: g.dma_start(
                        out=xfb[:], in_=x_d[b, tt * 128:(tt + 1) * 128, :]))(),
                        waits=[("sV", C("sV")), ("qG", wo_stores),
                               ("sT", C("sT"))], inc="qG")
                    DMA("gpsimd", (lambda b=b, tt=tt: lambda g: g.dma_start(
                        out=afb[:], in_=atD[b, tt * 128:(tt + 1) * 128, :]))(),
                        inc="qG")
                    DMA("gpsimd", lambda g: g.dma_start(
                        out=lnScr[0:1, 48:56], in_=lnScr[0:1, 56:64]), inc="qG")
                    ld = C("qG")
                    E("vector", lambda v: v.tensor_tensor(
                        out=ob[:], in0=xfb[:], in1=afb[:], op=OP.add),
                        waits=[("qG", ld), ("oW", C("oW"))], inc="sV")
                    E("vector", (lambda tt4=tt4: lambda v: v.tensor_tensor(
                        out=ob[:], in0=ffa[:, tt4, :], in1=ob[:], op=OP.add))(),
                        inc="sV", drain=True)
                    DMA("gpsimd", (lambda b=b, tt=tt: lambda g: g.dma_start(
                        out=out_d[b, tt * 128:(tt + 1) * 128, :], in_=ob[:]))(),
                        waits=[("sV", C("sV"))], inc="oW")

        def dump(nm, src):
            DMA("gpsimd", (lambda nm=nm, src=src: lambda g: g.dma_start(
                out=dbg[nm], in_=src))(),
                waits=[("sV", C("sV")), ("sA", C("sA")), ("sT", C("sT")),
                       ("qS", C("qS")), ("qG", C("qG")), ("qA", C("qA")),
                       ("dqw16", C("dqw16"))],
                inc="oW")

        import os as _os
        FEEDMASK = set((_os.environ.get("KFEED") or "qkv,norms,pass2,wo").split(","))
        nheads = H if "att" in phases else (1 if "att1" in phases else 0)
        state["dq_feed"] = None
        if "deq" in phases:
            fd, fn = dq_make_feeder()
            state["dq_feed"] = fd
            state["dq_finish"] = fn
        for b in range(1 if "b0" in phases else BC):
            if "ln" in phases:
                sch.phase = "ln%d" % b
                phase_ln(b)
                if "lnstats" in debug and b == 0:
                    dump("lnT", lnT[:])
                    dump("lnR", lnR[:])
                    dump("lnS", lnS[:])
                    dump("lnQ", lnQ[:])
                if "ln" in debug and b == 0:
                    dump("xnT", xnT[:])
                    dump("xnt", xnt_t[:])
                    dump("lnT", lnT[:])
                    dump("sdW", sdW[:])
                    dump("lnR", lnR[:])
                    dump("lnS", lnS[:])
                    dump("lnQ", lnQ[:])
            if nheads:
                sch.phase = "att%d" % b
                load_head_w(b, 0, 0)
            for h in range(nheads):
                if h + 1 < nheads:
                    load_head_w(b, h + 1, (h + 1) % 2)
                phase_qkv(b, h, h % 2, ((wq_h, qT), (wk_h, kT)))
                phase_qkv(b, h, h % 2, ((wv_h, vTt),))
                phase_norms(b, h)
                phase_vtok(b, h)
                if "qkv" in debug and b == 0 and h == 0:
                    dump("qT", qT[:])
                    dump("kT", kT[:])
                    dump("vtok", vtok[:])
                phase_pass1(b, h)
                if "p1" in debug and b == 0 and h == 0:
                    dump("rrow", rrow[:])
                    dump("negc", negc[:])
                phase_crow(b, h)
                phase_pass2_pv(b, h)
                if "p2" in debug and b == 0 and h == 0:
                    dump("ptb", ptb[:])
            if "wo" in phases:
                sch.phase = "wo%d" % b
                phase_wo(b)
                if "wo" in debug and b == 0:
                    DMA("gpsimd", lambda g: g.dma_start(out=dbg["att"], in_=atD[0]),
                        waits=[("qG", C("qG"))], inc="oW")
            if b == 0 and "deq" in phases:
                sch.phase = "deq"
                state["dq_finish"]()
                state["dq_feed"] = None
                if "deq" in debug:
                    DMA("gpsimd", lambda g: g.dma_start(out=dbg["wg0"], in_=wgC[0]),
                        waits=[("dqw16", C("dqw16"))], inc="oW")
                    DMA("gpsimd", lambda g: g.dma_start(out=dbg["wd0"],
                                                        in_=wdC[0:128, :]),
                        inc="oW")
                    DMA("gpsimd", lambda g: g.dma_start(out=dbg["wg31"],
                                                        in_=wgC[31]), inc="oW")
                    DMA("gpsimd", lambda g: g.dma_start(out=dbg["wu31"],
                                                        in_=wuC[31]), inc="oW")
                    DMA("gpsimd", lambda g: g.dma_start(
                        out=dbg["wd63"][0:128, :], in_=wdC[8064:8192, :]),
                        inc="oW")
                    DMA("gpsimd", lambda g: g.dma_start(
                        out=dbg["wd63"][128:256, :], in_=wdC[16256:16384, :]),
                        inc="oW")
            if "ffn" in phases:
                sch.phase = "ffn%d" % b
                phase_ffn(b)
                if "ffn" in debug and b == 0:
                    dump("intq", intq[:])
                    dump("ffa", ffa[:])

        E("gpsimd", lambda g: g.nop(), waits=[("oW", sch.count("oW")),
           ("qG", C("qG")), ("qA", C("qA")), ("qS", C("qS")),
           ("dqw16", C("dqw16"))])

        block = ctx(nc.Block())

        SPC = {"vector": 1024, "scalar": 1024, "gpsimd": 256}
        import os
        SIMMODE = bool(os.environ.get("KSIM"))

        PROF = bool(os.environ.get("KPROF"))

        def runner(name):
            def go(eng):
                seen = {}
                cur = [None, None]
                for (fn, waits, inc, incn, drn, pre, pad, ph) in sch.ops[name]:
                    if PROF and ph != cur[0]:
                        if cur[0] is not None:
                            nc.leave_named_scope(cur[0], cur[1], False)
                        cur[0] = ph
                        cur[1] = nc.enter_named_scope(ph, False)[0]
                    for (s, v) in waits:
                        if seen.get(s, -1) < v:
                            eng.wait_ge(sems[s], v)
                            seen[s] = v
                    if pre and not SIMMODE:
                        eng.nop(cycle_cnt=SPC.get(name, 256))
                    inst = fn(eng)
                    if drn:
                        if SIMMODE:
                            sp = eng.drain(fusable=False)
                        else:
                            eng.drain(fusable=False)
                            sp = eng.nop(cycle_cnt=SPC.get(name, 256))
                        if inc is not None:
                            sp.then_inc(sems[inc], incn)
                    elif inc is not None:
                        inst.then_inc(sems[inc], incn)
                    if pad and not SIMMODE:
                        eng.nop(cycle_cnt=64)
                if PROF and cur[0] is not None:
                    nc.leave_named_scope(cur[0], cur[1], False)
            return go

        block.sync(runner("sync"))
        block.scalar(runner("scalar"))
        block.vector(runner("vector"))
        block.tensor(runner("tensor"))
        block.gpsimd(runner("gpsimd"))

    return nc


def _host_prep(inputs):
    x = np.ascontiguousarray(np.asarray(inputs["L0_attn_x"], dtype=np.float32))
    dperm = np.concatenate([np.arange(0, D, 2), np.arange(1, D, 2)])
    nperm = np.concatenate([np.arange(0, NF, 2), np.arange(1, NF, 2)])

    def wT(w):
        return np.ascontiguousarray(np.asarray(w, dtype=np.float32).T[dperm])

    wq = wT(inputs["L0_attn_Wq"])
    wk = wT(inputs["L0_attn_Wk"])
    wv = wT(inputs["L0_attn_Wv"])
    wo = np.ascontiguousarray(np.asarray(inputs["L0_attn_Wo"], dtype=np.float32).T)

    def bytes_of(p):
        a = np.ascontiguousarray(np.asarray(p, dtype=np.int32))
        return a.view(np.uint8).reshape(a.shape[0], a.shape[1], 4)[:, :, 0]

    wgB = np.ascontiguousarray(bytes_of(inputs["L0_ffn_gate_W_packed"]).T[:, nperm])
    wuB = np.ascontiguousarray(bytes_of(inputs["L0_ffn_up_W_packed"]).T[:, nperm])
    wdB = np.ascontiguousarray(bytes_of(inputs["L0_ffn_down_W_packed"]).T)
    sgE = np.ascontiguousarray(
        np.asarray(inputs["L0_ffn_gate_scales"], dtype=np.float32).T[:, nperm])
    suE = np.ascontiguousarray(
        np.asarray(inputs["L0_ffn_up_scales"], dtype=np.float32).T[:, nperm])
    sdE = np.ascontiguousarray(
        np.asarray(inputs["L0_ffn_down_scales"], dtype=np.float32).T)
    idf = np.eye(128, dtype=np.float32)
    shared = dict(wqT=wq, wkT=wk, wvT=wv, woT=wo, wgB=wgB, sgE=sgE,
                  wuB=wuB, suE=suE, wdB=wdB, sdE=sdE, idf=idf)
    return x, shared


def kernel(**inputs):
    x, shared = _host_prep(inputs)
    nc = build_nc()
    in_maps = []
    for c in range(NCORES):
        m = dict(shared)
        m["x"] = np.ascontiguousarray(x[c * BC:(c + 1) * BC])
        in_maps.append(m)
    res = run_bass_kernel_spmd(nc, in_maps, core_ids=list(range(NCORES)))
    out = np.concatenate([r["out"] for r in res.results], axis=0)
    return out


# version 15 (final)
# speedup vs baseline: 1.1754x; 1.1754x over previous
import numpy as np
import ml_dtypes
import concourse.bass as bass
import concourse.mybir as mybir
from concourse.bass_utils import run_bass_kernel_spmd
from contextlib import ExitStack

F32 = mybir.dt.float32
BF16 = mybir.dt.bfloat16
U8 = mybir.dt.uint8
I32 = mybir.dt.int32
AF = mybir.ActivationFunctionType
OP = mybir.AluOpType
AX = mybir.AxisListType.X

B, T, D = 16, 2048, 2048
H, DH = 16, 128
NF = 16384
NCORES = 8
BC = B // NCORES
KC = D // 128
TT = T // 128
TCH = T // 512
SOFT = 1.0 / float(np.sqrt(DH))
EPS = 1e-5
P126 = float(2.0 ** 126)

GU_CC = 8
NSTR = NF // 512
DN_CC = 64
HST = D // 512
NQ = 4
NCHQ = 32


class Sched:

    def __init__(self):
        self.ops = {"sync": [], "scalar": [], "vector": [], "tensor": [], "gpsimd": []}
        self.counts = {}
        self.phase = "init" 

    def count(self, sem):
        return self.counts.get(sem, 0)

    def emit(self, eng, fn, waits=(), inc=None, incn=1, drain=False, prenop=False,
             pad=False):
        ws = [(s, v) for (s, v) in waits if v > 0]
        self.ops[eng].append((fn, ws, inc, incn, drain, prenop, pad, self.phase))
        if inc is not None:
            self.counts[inc] = self.counts.get(inc, 0) + incn

    def dma(self, eng, fn, waits=(), inc=None):
        self.emit(eng, fn, waits, inc, 16)


LN_TILES = TT


def build_nc(phases=("ln", "att", "wo", "deq", "ffn"), debug=()):
    nc = bass.Bass("TRN2", target_bir_lowering=False)

    x_d = nc.dram_tensor("x", [BC, T, D], F32, kind="ExternalInput").ap()
    wqT = nc.dram_tensor("wqT", [D, D], F32, kind="ExternalInput").ap()
    wkT = nc.dram_tensor("wkT", [D, D], F32, kind="ExternalInput").ap()
    wvT = nc.dram_tensor("wvT", [D, D], F32, kind="ExternalInput").ap()
    woT = nc.dram_tensor("woT", [D, D], F32, kind="ExternalInput").ap()
    wgB = nc.dram_tensor("wgB", [1024, NF], U8, kind="ExternalInput").ap()
    sgE = nc.dram_tensor("sgE", [64, NF], F32, kind="ExternalInput").ap()
    wuB = nc.dram_tensor("wuB", [1024, NF], U8, kind="ExternalInput").ap()
    suE = nc.dram_tensor("suE", [64, NF], F32, kind="ExternalInput").ap()
    wdB = nc.dram_tensor("wdB", [NF // 2, D], U8, kind="ExternalInput").ap()
    sdE = nc.dram_tensor("sdE", [512, D], F32, kind="ExternalInput").ap()
    idf = nc.dram_tensor("idf", [128, 128], F32, kind="ExternalInput").ap()
    out_d = nc.dram_tensor("out", [BC, T, D], F32, kind="ExternalOutput").ap()
    dbg = {}
    for nm, shp, ddt in (("xnT", [128, KC, T], BF16), ("qT", [128, T], BF16),
                         ("kT", [128, T], BF16), ("vtok", [128, KC, 128], BF16),
                         ("rrow", [1, T], F32), ("negc", [128, 16], F32),
                         ("ptb", [128, KC, 512], BF16), ("att", [T, D], F32),
                         ("wg0", [KC, 128, 512], BF16), ("wd0", [128, D], BF16),
                         ("intq", [128, NCHQ, 512], BF16), ("ffa", [128, 4, D], F32),
                         ("lnst", [128, 12], F32), ("xnt", [128, 2, D], BF16),
                         ("lnT", [128, 16], F32), ("sdW", [128, 16], F32),
                         ("wgs", [128, 16, 256], BF16), ("gg", [128, 512], BF16),
                         ("wds", [128, 4, 512], BF16),
                         ("wg31", [KC, 128, 512], BF16),
                         ("wu31", [KC, 128, 512], BF16),
                         ("wd63", [256, D], BF16),
                         ("lnR", [128, 16], F32), ("lnS", [128, 16], F32),
                         ("lnQ", [128, 16], F32)):
        dbg[nm] = nc.dram_tensor("dbg_" + nm, shp, ddt, kind="ExternalOutput").ap()

    wgC = nc.dram_tensor("wgC", [NSTR, KC, 128, 512], BF16).ap()
    wuC = nc.dram_tensor("wuC", [NSTR, KC, 128, 512], BF16).ap()
    wdC = nc.dram_tensor("wdC", [NF, D], BF16).ap()
    lnScr = nc.dram_tensor("lnScr", [128, 64], F32).ap()
    aM4 = nc.dram_tensor("aM4", [128, 1], F32).ap()
    aNegc = nc.dram_tensor("aNegc", [128, 16], F32).ap()
    aC0 = nc.dram_tensor("aC0", [1, 1], F32).ap()
    aRQ = nc.dram_tensor("aRQ", [128, 16], F32).ap()
    aScr2h = nc.dram_tensor("aScr2h", [16, 128], BF16).ap()
    aScr2l = nc.dram_tensor("aScr2l", [16, 128], BF16).ap()
    otD = nc.dram_tensor("otD", [BC, D, T], BF16).ap()
    atD = nc.dram_tensor("atD", [BC, T, D], F32).ap()

    sch = Sched()
    with ExitStack() as st:
        ctx = st.enter_context
        sb = {}

        def S(name, shape, dt):
            sb[name] = ctx(nc.sbuf_tensor(name, shape, dt))
            return sb[name]

        xnT = S("xnT", [128, KC, T], BF16)
        identb = S("identb", [128, 128], BF16)
        identf = S("identf", [128, 128], F32)
        ones_c = S("ones_c", [128, 1], BF16)
        ones_r = S("ones_r", [1, 128], BF16)
        ones_f = S("ones_f", [128, 1], F32)
        lnst = S("lnst", [128, 12], F32)
        nrm = S("nrm", [128, 48], F32)
        rbuf = S("rbuf", [128, 64], F32)
        rQ = S("rQ", [128, 16], F32)
        rowsT = S("rowsT", [16, 256], F32)
        c2h16 = S("c2h16", [16, 128], BF16)
        c2l16 = S("c2l16", [16, 128], BF16)
        kapb = S("kapb", [1, 2], BF16)
        c0b = S("c0b", [128, 1], F32)
        rr16 = S("rr16", [1, 512], BF16)
        rrf = S("rrf", [1, 512], F32)
        rbc = S("rbc", [128, 512], F32)
        flA = S("flA", [128, 2], F32)
        negc = S("negc", [128, 16], F32)
        epst = S("epst", [128, 1], F32)
        lnS = S("lnS", [128, 16], F32)
        lnQ = S("lnQ", [128, 16], F32)
        lnM = S("lnM", [128, 16], F32)
        lnV = S("lnV", [128, 16], F32)
        lnT = S("lnT", [128, 16], F32)
        sdW = S("sdW", [128, 16], F32)
        lnR = S("lnR", [128, 16], F32)
        ones16 = S("ones16", [128, 16], F32)
        flT = S("flT", [128, 4], F32)
        pdump = S("pdump", [128, 512], BF16)
        otb = S("otb", [128, 2, 512], BF16)

        ARENA = 131072
        arena = ctx(nc.sbuf_tensor("arena", [128, ARENA], U8))
        abase = nc.lookup_mloc(arena).addr

        def AT(name, shape, dt, off):
            h = nc.alloc_sbuf_tensor_at(name, shape, dt, offset=abase + off)
            sb[name] = h
            return h

        xbuf = AT("xbuf", [128, 2, D], F32, 65536)
        xsq = AT("xsq", [128, D], F32, 81920)
        xnt_t = AT("xnt_t", [128, 2, D], BF16, 90112)
        wq_h = AT("wq_h", [128, 2, KC, 128], BF16, 0)
        wk_h = AT("wk_h", [128, 2, KC, 128], BF16, 8192)
        wv_h = AT("wv_h", [128, 2, KC, 128], BF16, 16384)
        qT = AT("qT", [128, T], BF16, 24576)
        kT = AT("kT", [128, T], BF16, 28672)
        vTt = AT("vTt", [128, T], BF16, 32768)
        vtok = AT("vtok", [128, KC, 128], BF16, 36864)
        sq2 = AT("sq2", [128, T], BF16, 40960)
        ptb = AT("ptb", [128, KC, 512], BF16, 45056)
        dqs = AT("dqs", [128, 2, 512], F32, 98304)
        dqi = AT("dqi", [128, 2, 512], I32, 102400)
        dqp = AT("dqp", [128, 512], I32, 106496)
        dqt2 = AT("dqt2", [128, 512], I32, 108544)
        dqu = AT("dqu", [128, 2, 512], I32, 110592)
        dqw = AT("dqw", [128, 8, 2, 512], BF16, 114688)
        rrow = AT("rrow", [1, T], F32, 80896)
        ncrow = AT("ncrow", [1, T], F32, 89088)
        crsc = AT("crsc", [1, T], F32, 97280)
        c2hi = AT("c2hi", [1, T], BF16, 105472)
        c2lo = AT("c2lo", [1, T], BF16, 109568)
        woS = AT("woS", [128, KC, D], BF16, 0)
        otl = AT("otl", [128, KC * 128], BF16, 65536)
        att_s = AT("att_s", [128, 2, D], F32, 69632)
        wgs = AT("wgs", [128, 2, KC, 256], BF16, 0)
        wus = AT("wus", [128, 2, KC, 256], BF16, 16384)
        wds = AT("wds", [128, 4, 512], BF16, 32768)
        gg = AT("gg", [128, 2, 512], BF16, 36864)
        intq = AT("intq", [128, NCHQ, 512], BF16, 40960)
        ffa = AT("ffa", [128, 4, D], F32, 73728)
        xfb = AT("xfb", [128, D], F32, 106496)
        afb = AT("afb", [128, D], F32, 114688)
        ob = AT("ob", [128, D], F32, 122880)

        ps = {}
        for nm in "ABCDEF":
            ps[nm] = ctx(nc.psum_tensor("ps" + nm, [128, 512], F32))
        psTR = ctx(nc.psum_tensor("psTR", [128, 1024], BF16))
        psR = ctx(nc.psum_tensor("psR", [128, 512], F32))

        sem_names = ["qS", "qW", "qA", "qG", "qE", "sT", "sV", "sA", "dqw16", "oW"]
        sems = {n: ctx(nc.semaphore(n)) for n in sem_names}

        E = sch.emit
        DMA = sch.dma
        C = sch.count

        def TGUARD(waits=()):
            E("tensor", lambda t: t.transpose(psR[0:1, 384:512], epst[:],
                                              identf[:]), waits=waits)

        def AFLUSH(waits=()):
            E("scalar", lambda sc: sc.activation(
                out=flA[:, 0:1], in_=epst[:], func=AF.Exp,
                accum_out=flA[:, 1:2]), waits=waits)

        def VFLUSH(waits=()):
            E("vector", lambda v: v.tensor_reduce(
                out=flT[:, 0:1], in_=ones16[:, 0:4], axis=AX, op=OP.add),
                waits=waits)
        state = {}

        DMA("gpsimd", lambda g: g.dma_start(out=identf[:], in_=idf), inc="qG")
        E("vector", lambda v: v.tensor_copy(identb[:], identf[:]),
          waits=[("qG", 16)], inc="sV", drain=True)
        E("vector", lambda v: v.memset(ones_c[:], 1.0), inc="sV", drain=True)
        E("vector", lambda v: v.memset(ones_r[:], 1.0), inc="sV", drain=True)
        E("vector", lambda v: v.memset(ones_f[:], 1.0), inc="sV", drain=True)
        E("vector", lambda v: v.memset(epst[:], EPS), inc="sV", drain=True)
        E("vector", lambda v: v.memset(ones16[:], 1.0), inc="sV", drain=True)

        def phase_ln(b):
            simple = "lnsimple" in phases
            for i in range(LN_TILES):
                sl = i % 2
                DMA("scalar", (lambda b=b, i=i, sl=sl: lambda e: e.dma_start(
                    out=xbuf[:, sl, :], in_=x_d[b, i * 128:(i + 1) * 128, :]))(),
                    waits=[("sV", C("sV")), ("sA", C("sA")), ("sT", C("sT")),
                           ("oW", C("oW")), ("qS", C("qS"))], inc="qA")
                wl = [("qA", C("qA"))]
                E("vector", (lambda i=i, sl=sl: lambda v: v.tensor_scalar(
                    out=xsq[:], in0=xbuf[:, sl, :], scalar1=1.0, scalar2=0.0,
                    op0=OP.mult, op1=OP.add, accum_out=lnS[:, i:i + 1]))(),
                    waits=wl)
                E("vector", (lambda i=i, sl=sl: lambda v: v.scalar_tensor_tensor(
                    out=xsq[:], in0=xbuf[:, sl, :], scalar=1.0,
                    in1=xbuf[:, sl, :], op0=OP.mult, op1=OP.mult,
                    accum_out=lnQ[:, i:i + 1]))())
                VFLUSH()
                E("vector", lambda v: v.engine_nop(), inc="sV", drain=True)
            acc_done = C("sV")
            DMA("gpsimd", lambda g: g.dma_start(out=lnScr[:, 0:16], in_=lnS[:]),
                waits=[("sV", acc_done)], inc="qG")
            DMA("gpsimd", lambda g: g.dma_start(out=lnScr[:, 16:32], in_=lnQ[:]),
                inc="qG")
            rt1o = C("qG")
            DMA("gpsimd", lambda g: g.dma_start(out=lnS[:], in_=lnScr[:, 0:16]),
                waits=[("qG", rt1o)], inc="qG")
            DMA("gpsimd", lambda g: g.dma_start(out=lnQ[:], in_=lnScr[:, 16:32]),
                inc="qG")
            rt1 = C("qG")
            E("vector", lambda v: v.tensor_scalar(
                out=lnM[:], in0=lnS[:], scalar1=-1.0 / D, scalar2=None,
                op0=OP.mult), waits=[("qG", rt1)], pad=True)
            E("vector", lambda v: v.tensor_scalar(
                out=lnV[:], in0=lnQ[:], scalar1=1.0 / D, scalar2=None, op0=OP.mult),
                pad=True)
            E("vector", lambda v: v.tensor_tensor(
                out=sdW[:], in0=lnM[:], in1=lnM[:], op=OP.mult), pad=True)
            E("vector", lambda v: v.tensor_tensor(
                out=lnT[:], in0=lnV[:], in1=sdW[:], op=OP.subtract),
                inc="sV", drain=True)
            DMA("gpsimd", lambda g: g.dma_start(out=lnScr[:, 32:48], in_=lnT[:]),
                waits=[("sV", C("sV"))], inc="qG")
            DMA("gpsimd", lambda g: g.dma_start(out=lnT[:], in_=lnScr[:, 32:48]),
                waits=[("qG", C("qG"))], inc="qG")
            rt2 = C("qG")
            E("scalar", lambda sc: sc.activation(
                out=sdW[:], in_=lnT[:], func=AF.Sqrt, bias=epst[:]),
                waits=[("qG", rt2)], inc="sA", drain=True)
            DMA("gpsimd", lambda g: g.dma_start(out=lnScr[:, 48:64], in_=sdW[:]),
                waits=[("sA", C("sA"))], inc="qG")
            DMA("gpsimd", lambda g: g.dma_start(out=sdW[:], in_=lnScr[:, 48:64]),
                waits=[("qG", C("qG"))], inc="qG")
            rt3 = C("qG")
            E("vector", lambda v: v.reciprocal(lnR[:], sdW[:]),
                waits=[("qG", rt3)], inc="sV", drain=True)
            loads = []
            for i in range(LN_TILES):
                sl = i % 2
                wl3 = [("sV", C("sV")), ("qS", C("qS"))]
                DMA("scalar", (lambda b=b, i=i, sl=sl: lambda e: e.dma_start(
                    out=xbuf[:, sl, :], in_=x_d[b, i * 128:(i + 1) * 128, :]))(),
                    waits=wl3, inc="qA")
                ld = C("qA")
                pview = xnt_t[:, sl, :].rearrange("p (two m) -> p m two", two=2)
                if simple:
                    E("vector", (lambda sl=sl, pview=pview: lambda v: v.tensor_scalar(
                        out=pview, in0=xbuf[:, sl, :], scalar1=1.0,
                        scalar2=None, op0=OP.mult))(),
                        waits=[("qA", ld)], inc="sV", drain=True)
                else:
                    E("vector", (lambda i=i, sl=sl, pview=pview: lambda v:
                        v.scalar_tensor_tensor(
                            out=pview, in0=xbuf[:, sl, :],
                            scalar=lnM[:, i:i + 1],
                            in1=lnR[:, i:i + 1].broadcast_to([128, D]),
                            op0=OP.add, op1=OP.mult))(),
                        waits=[("qA", ld)], inc="sV", drain=True)
                stt_done = C("sV")
                for c in range(KC):
                    DMA("sync", (lambda sl=sl, c=c, i=i: lambda e: e.dma_start(
                        out=xnT[:, c, i * 128:(i + 1) * 128],
                        in_=xnt_t[:, sl, c * 128:(c + 1) * 128],
                        transpose=True))(),
                        waits=[("sV", stt_done)], inc="qS")

        def load_head_w(b, h, sl):
            bar = [("sT", C("sT")), ("sV", C("sV")), ("sA", C("sA")),
                   ("oW", C("oW"))]
            for (w_d, dst) in ((wqT, wq_h), (wkT, wk_h), (wvT, wv_h)):
                for g4 in range(4):
                    DMA("gpsimd", (lambda w_d=w_d, dst=dst, sl=sl, h=h, g4=g4:
                        lambda g: g.dma_start(
                            out=dst[:, sl, g4 * 4:(g4 + 1) * 4, :],
                            in_=w_d.rearrange("(c p) d -> p c d", p=128)[
                                :, g4 * 4:(g4 + 1) * 4,
                                h * 128:(h + 1) * 128]))(),
                        waits=bar, inc="qG")

        def phase_qkv(b, h, sl, projs):
            qg_done = C("qG")
            for (wbuf, obuf) in projs:
                for tc in range(TCH):
                    bank = "E" if tc % 2 == 0 else "F"
                    for c in range(KC):
                        wt = [("qG", qg_done)]
                        if c == 0:
                            wt += [("sV", C("sV")), ("sA", C("sA"))]
                        E("tensor", (lambda wbuf=wbuf, sl=sl, c=c, tc=tc, bank=bank:
                            lambda t: t.matmul(
                                ps[bank][:], wbuf[:, sl, c, :],
                                xnT[:, c, tc * 512:(tc + 1) * 512],
                                start=(c == 0), stop=(c == KC - 1)))(),
                            waits=wt, inc="sT")
                    E("vector", (lambda obuf=obuf, tc=tc, bank=bank: lambda v:
                        v.tensor_copy(obuf[:, tc * 512:(tc + 1) * 512], ps[bank][:]))(),
                        waits=[("sT", C("sT")), ("sA", C("sA"))], inc="sV",
                        drain=True)
                    if (b == 0 and state.get("dq_feed") and "qkv" in FEEDMASK
                            and tc == 3):
                        state["dq_feed"](1)

        def phase_vtok(b, h):
            vt_done = C("sV")
            for c in range(KC):
                slot = c % 8
                E("tensor", (lambda c=c, slot=slot: lambda t: t.transpose(
                    psTR[:, slot * 128:(slot + 1) * 128],
                    vTt[:, c * 128:(c + 1) * 128], identb[:]))(),
                    waits=[("sV", vt_done if c < 8 else C("sV") - 7)], inc="sT")
                E("vector", (lambda c=c, slot=slot: lambda v: v.tensor_copy(
                    vtok[:, c, :], psTR[:, slot * 128:(slot + 1) * 128]))(),
                    waits=[("sT", C("sT"))], inc="sV", prenop=True)

        def phase_norms(b, h):
            qk_done = C("sV")
            E("scalar", lambda sc: sc.activation(out=sq2[:], in_=kT[:],
                                                 func=AF.Square),
              waits=[("sV", qk_done)], inc="sA", drain=True)
            TGUARD(waits=[("sA", C("sA"))])
            for j in range(TT):
                E("tensor", (lambda j=j: lambda t: t.matmul(
                    psR[:, j:j + 1], sq2[:, j * 128:(j + 1) * 128], ones_c[:],
                    start=True, stop=True))(), inc="sT")
            if b == 0 and state.get("dq_feed") and "norms" in FEEDMASK:
                state["dq_feed"](2)
            E("vector", lambda v: v.tensor_copy(nrm[:, 0:16], psR[:, 0:16]),
              waits=[("sT", C("sT"))], prenop=True, pad=True)
            E("vector", lambda v: v.tensor_scalar(
                out=rbuf[:, 0:16], in0=nrm[:, 0:16], scalar1=1.0, scalar2=-3.0e38,
                op0=OP.mult, op1=OP.max, accum_out=nrm[:, 16:17]))
            VFLUSH()
            E("vector", lambda v: v.tensor_tensor(
                out=nrm[:, 17:18], in0=nrm[:, 16:17], in1=nrm[:, 16:17],
                op=OP.mult), pad=True)
            E("vector", lambda v: v.tensor_tensor(
                out=nrm[:, 18:19], in0=nrm[:, 17:18], in1=nrm[:, 17:18],
                op=OP.mult), inc="sV", drain=True)
            DMA("gpsimd", lambda g: g.dma_start(out=aM4,
                                                in_=nrm[:, 18:19]),
                waits=[("sV", C("sV"))], inc="qG")
            DMA("gpsimd", lambda g: g.dma_start(out=nrm[:, 18:19],
                                                in_=aM4),
                waits=[("qG", C("qG"))], inc="qG")
            E("tensor", lambda t: t.matmul(psR[0:1, 20:21], nrm[:, 18:19],
                                           ones_f[:], start=True, stop=True),
              waits=[("qG", C("qG"))], inc="sT")
            E("scalar", lambda sc: sc.activation(out=flA[0:1, 0:1],
                                                 in_=psR[0:1, 20:21],
                                                 func=AF.Sqrt),
              waits=[("sT", C("sT"))], prenop=True, pad=True)
            E("scalar", lambda sc: sc.activation(out=flA[0:1, 0:1],
                                                 in_=flA[0:1, 0:1], func=AF.Sqrt),
              pad=True)
            E("scalar", lambda sc: sc.activation(out=kapb[0:1, 0:1],
                                                 in_=flA[0:1, 0:1], func=AF.Sqrt),
              inc="sA", drain=True)
            E("tensor", lambda t: t.matmul(psR[:, 22:23], ones_r[:],
                                           kapb[0:1, 0:1], start=True, stop=True),
              waits=[("sA", C("sA"))], inc="sT")
            E("vector", lambda v: v.tensor_copy(nrm[:, 19:20], psR[:, 22:23]),
              waits=[("sT", C("sT"))], inc="sV", prenop=True, pad=True, drain=True)
            if b == 0 and state.get("dq_feed") and "norms" in FEEDMASK:
                state["dq_feed"](2)
            E("scalar", lambda sc: sc.activation(out=sq2[:], in_=qT[:],
                                                 func=AF.Square),
              waits=[("sV", C("sV"))], inc="sA", drain=True)
            TGUARD(waits=[("sA", C("sA"))])
            for j in range(TT):
                E("tensor", (lambda j=j: lambda t: t.matmul(
                    psR[:, 24 + j:25 + j], sq2[:, j * 128:(j + 1) * 128],
                    ones_c[:], start=True, stop=True))(), inc="sT")
            E("vector", lambda v: v.tensor_copy(nrm[:, 24:40], psR[:, 24:40]),
              waits=[("sT", C("sT"))], inc="sV", prenop=True, pad=True, drain=True)
            E("scalar", lambda sc: sc.activation(out=nrm[:, 24:40],
                                                 in_=nrm[:, 24:40], func=AF.Sqrt),
              waits=[("sV", C("sV"))], inc="sA", drain=True)
            E("vector", lambda v: v.scalar_tensor_tensor(
                out=negc[:, 0:16], in0=nrm[:, 24:40], scalar=-SOFT,
                in1=nrm[:, 19:20].broadcast_to([128, 16]), op0=OP.mult,
                op1=OP.mult),
              waits=[("sA", C("sA"))], inc="sV", drain=True)
            DMA("gpsimd", lambda g: g.dma_start(out=aNegc,
                                                in_=negc[:, 0:16]),
                waits=[("sV", C("sV"))], inc="qG")
            DMA("gpsimd", lambda g: g.dma_start(out=negc[:, 0:16],
                                                in_=aNegc),
                waits=[("qG", C("qG"))], inc="qG")
            state["negc_rt"] = C("qG")
            E("vector", lambda v: v.tensor_scalar(
                out=rbuf[:, 0:16], in0=negc[:, 0:16], scalar1=1.0,
                scalar2=3.0e38, op0=OP.mult, op1=OP.min,
                accum_out=nrm[:, 41:42]),
              waits=[("qG", state["negc_rt"])])
            VFLUSH()
            E("vector", lambda v: v.engine_nop(), inc="sV", drain=True)
            DMA("gpsimd", lambda g: g.dma_start(out=aM4, in_=nrm[:, 41:42]),
                waits=[("sV", C("sV"))], inc="qG")
            DMA("gpsimd", lambda g: g.dma_start(
                out=rowsT[0:1, 0:128],
                in_=aM4.rearrange("a b -> (a b)").unsqueeze(0)),
                waits=[("qG", C("qG"))], inc="qG")
            E("vector", lambda v: v.tensor_scalar(
                out=rowsT[0:1, 128:256], in0=rowsT[0:1, 0:128], scalar1=1.0,
                scalar2=3.0e38, op0=OP.mult, op1=OP.min,
                accum_out=rowsT[0:1, 200:201]),
              waits=[("qG", C("qG"))], pad=True)
            VFLUSH()
            E("vector", lambda v: v.engine_nop(), inc="sV", drain=True)
            DMA("gpsimd", lambda g: g.dma_start(out=aC0, in_=rowsT[0:1, 200:201]),
                waits=[("sV", C("sV"))], inc="qG")
            DMA("gpsimd", lambda g: g.dma_start(
                out=c0b[:], in_=aC0.broadcast_to([128, 1])),
                waits=[("qG", C("qG"))], inc="qG")
            state["c0_rt"] = C("qG")
            if b == 0 and state.get("dq_feed") and "norms" in FEEDMASK:
                state["dq_feed"](3)

        def phase_pass1(b, h):
            pass

        def phase_crow(b, h):
            pass

        def phase_pass2_pv(b, h):
            c0_rt = state["c0_rt"]
            for qc in range(TCH):
                for kt in range(KC):
                    bank = "A" if kt % 2 == 0 else "B"
                    wt = [("qG", c0_rt)]
                    if kt >= 2 or qc > 0:
                        wt.append(("sA", C("sA") - 1 if kt >= 2 else C("sA")))
                    E("tensor", (lambda kt=kt, qc=qc, bank=bank: lambda t: t.matmul(
                        ps[bank][:], kT[:, kt * 128:(kt + 1) * 128],
                        qT[:, qc * 512:(qc + 1) * 512],
                        start=True, stop=True))(), waits=wt, inc="sT")
                    E("scalar", (lambda kt=kt, bank=bank: lambda s: s.activation(
                        out=ptb[:, kt, :], in_=ps[bank][:], func=AF.Exp,
                        bias=c0b[:, 0:1], scale=SOFT))(),
                        waits=[("sT", C("sT"))], inc="sA", drain=(kt == KC - 1))
                if b == 0 and state.get("dq_feed") and "pass2" in FEEDMASK:
                    state["dq_feed"](3)
                pt_done = C("sA")
                for kt in range(KC):
                    wt = [("sA", pt_done)]
                    if kt == 0:
                        wt.append(("sV", C("sV")))
                    E("tensor", (lambda kt=kt: lambda t: t.matmul(
                        ps["C"][:], vtok[:, kt, :], ptb[:, kt, :],
                        start=(kt == 0), stop=(kt == KC - 1)))(),
                        waits=wt, inc="sT")
                for kt in range(KC):
                    wt = [("sA", pt_done)]
                    if kt == 0:
                        wt.append(("sV", C("sV")))
                    E("tensor", (lambda kt=kt: lambda t: t.matmul(
                        ps["D"][0:1, :], ones_c[:, 0:1], ptb[:, kt, :],
                        start=(kt == 0), stop=(kt == KC - 1)))(),
                        waits=wt, inc="sT")
                E("vector", lambda v: v.reciprocal(rrf[0:1, :], ps["D"][0:1, :]),
                  waits=[("sT", C("sT"))], prenop=True, pad=True)
                E("vector", lambda v: v.tensor_copy(rr16[0:1, :], rrf[0:1, :]),
                  inc="sV", pad=True, drain=True)
                E("tensor", lambda t: t.matmul(ps["E"][:], ones_r[:],
                                               rr16[0:1, :],
                                               start=True, stop=True),
                  waits=[("sV", C("sV"))], inc="sT")
                E("vector", lambda v: v.tensor_copy(rbc[:], ps["E"][:]),
                  waits=[("sT", C("sT"))])
                osl = qc % 2
                E("vector", (lambda osl=osl: lambda v: v.tensor_tensor(
                    out=otb[:, osl, :], in0=ps["C"][:], in1=rbc[:],
                    op=OP.mult))(),
                    waits=[("qG", C("qG"))], inc="sV",
                    drain=True)
                DMA("gpsimd", (lambda b=b, h=h, qc=qc, osl=osl: lambda g: g.dma_start(
                    out=otD[b, h * 128:(h + 1) * 128, qc * 512:(qc + 1) * 512],
                    in_=otb[:, osl, :]))(),
                    waits=[("sV", C("sV"))], inc="qG")

        def phase_wo(b):
            for grp in range(4):
                DMA("gpsimd", (lambda grp=grp: lambda g: g.dma_start(
                    out=woS[:, grp * 4:(grp + 1) * 4, :],
                    in_=woT.rearrange("(c p) d -> p c d", p=128)[
                        :, grp * 4:(grp + 1) * 4, :]))(),
                    waits=[("sT", C("sT")), ("sA", C("sA")), ("sV", C("sV"))],
                    inc="qG")
            wo_done = C("qG")
            for tt in range(TT):
                DMA("scalar", (lambda b=b, tt=tt: lambda e: e.dma_start(
                    out=otl[:].rearrange("p (c q) -> p c q", c=KC),
                    in_=otD[b].rearrange("(c p) t -> p c t", p=128)[
                        :, :, tt * 128:(tt + 1) * 128]))(),
                    waits=[("sT", C("sT")), ("qG", wo_done)], inc="qA")
                DMA("scalar", lambda e: e.dma_start(
                    out=lnScr[0:1, 32:40], in_=lnScr[0:1, 40:48]), inc="qA")
                ld = C("qA")
                TGUARD(waits=[("qA", ld)])
                banks = ["A", "B", "C", "D"]
                for c in range(KC):
                    for dsl in range(4):
                        wt = [("qA", ld)]
                        if c == 0:
                            wt.append(("sV", C("sV")))
                        E("tensor", (lambda c=c, dsl=dsl, bank=banks[dsl]:
                            lambda t: t.matmul(
                                ps[bank][:],
                                otl[:, c * 128:(c + 1) * 128],
                                woS[:, c, dsl * 512:(dsl + 1) * 512],
                                start=(c == 0), stop=(c == KC - 1)))(),
                            waits=wt, inc="sT")
                t_done = C("sT")
                esl = tt % 2
                for dsl in range(4):
                    E("vector", (lambda dsl=dsl, esl=esl, bank=banks[dsl]: lambda v:
                        v.tensor_copy(att_s[:, esl, dsl * 512:(dsl + 1) * 512],
                                      ps[bank][:]))(),
                        waits=[("sT", t_done), ("qG", C("qG"))], inc="sV",
                        drain=(dsl == 3))
                DMA("gpsimd", (lambda b=b, tt=tt, esl=esl: lambda g: g.dma_start(
                    out=atD[b, tt * 128:(tt + 1) * 128, :], in_=att_s[:, esl, :]))(),
                    waits=[("sV", C("sV"))], inc="qG")
                if b == 0 and state.get("dq_feed") and "wo" in FEEDMASK:
                    state["dq_feed"](2)
            state["wo_stores_%d" % b] = C("qG")

        def dequant_tile(w_in, s_in, cch, n0, store_fns, sl):
            w = 512
        DQW_SLOTS = 8
        DRB = 6

        def dq_loads(tile, ti, dq_state):
            (w_in, s_in, cch, n0, _sf) = tile
            w = 512
            s2 = ti % 2
            sv_after = dq_state["sv_after"]
            wl = []
            if ti < 2:
                wl += dq_state.get("bar", [])
            if ti >= 2 and (ti - 2) in sv_after:
                wl.append(("sV", sv_after[ti - 2]))
            DMA("gpsimd", (lambda w_in=w_in, cch=cch, n0=n0, s2=s2: lambda e:
                e.dma_start(out=dqi[:, s2, :],
                            in_=w_in[cch * 128:(cch + 1) * 128, n0:n0 + w]))(),
                waits=wl, inc="qG")
            DMA("scalar", (lambda s_in=s_in, cch=cch, n0=n0, s2=s2: lambda e:
                e.dma_start(out=dqs[:, s2, :],
                            in_=s_in[cch * 8:(cch + 1) * 8, n0:n0 + w]
                            .unsqueeze(1).broadcast_to([8, 16, w])))(),
                waits=wl, inc="qA")
            return (C("qG"), C("qA"))

        def dequant_tile(store_fns, ti, succ, dq_state):
            s2 = ti % 2
            sl = ti % DQW_SLOTS
            sv_after = dq_state["sv_after"]
            st_cum = dq_state["st_cum"]
            wl2 = [("qG", succ[0]), ("qA", succ[1])]
            if ti < 2:
                wl2 += dq_state.get("bar", [])
            if ti >= DQW_SLOTS:
                wl2.append(("dqw16", st_cum[ti - DQW_SLOTS]))
            E("vector", (lambda s2=s2: lambda v: v.tensor_scalar(
                out=dqs[:, s2, :], in0=dqs[:, s2, :], scalar1=P126,
                scalar2=None, op0=OP.mult))(), waits=wl2)
            for lh, (m1, sh1, m2, sh2) in enumerate(((8, 12, 7, 6),
                                                     (128, 8, 112, 2))):
                E("vector", (lambda s2=s2, m1=m1, sh1=sh1: lambda v: v.tensor_scalar(
                    out=dqt2[:], in0=dqi[:, s2, :], scalar1=m1, scalar2=sh1,
                    op0=OP.bitwise_and, op1=OP.logical_shift_left))())
                E("vector", (lambda s2=s2, m2=m2, sh2=sh2: lambda v: v.tensor_scalar(
                    out=dqp[:], in0=dqi[:, s2, :], scalar1=m2, scalar2=sh2,
                    op0=OP.bitwise_and, op1=OP.logical_shift_left))())
                E("vector", (lambda lh=lh: lambda v: v.tensor_tensor(
                    out=dqu[:, lh, :], in0=dqt2[:], in1=dqp[:],
                    op=OP.bitwise_or))())
            for lh in range(2):
                E("vector", (lambda lh=lh, sl=sl, s2=s2: lambda v: v.tensor_tensor(
                    out=dqw[:, sl, lh, :],
                    in0=dqu[:, lh, :].bitcast(BF16)[:, 0:2 * 512:2],
                    in1=dqs[:, s2, :], op=OP.mult))(), inc="sV")
            sv_after[ti] = C("sV")
            dq_state["pend"].append((store_fns, sl, ti))

        def dq_flush(dq_state):
            if not dq_state["pend"]:
                return
            E("vector", lambda v: v.engine_nop(), inc="sV", drain=True)
            ms = C("sV")
            for (sfns, sl, ti) in dq_state["pend"]:
                for lh in range(2):
                    DMA("sync", (lambda fn=sfns[lh], sl=sl, lh=lh: lambda e:
                        fn(e, dqw[:, sl, lh, :]))(),
                        waits=[("sV", ms)], inc="dqw16")
                dq_state["st_cum"][ti] = C("dqw16")
            dq_state["pend"] = []

        def dq_tile_list():
            tiles = []
            for (w_in, s_in, cache) in ((wgB, sgE, wgC), (wuB, suE, wuC)):
                for strip in range(NSTR):
                    for cch in range(GU_CC):
                        sf = []
                        for lh in range(2):
                            kc = cch + 8 * lh

                            def mk(cache=cache, strip=strip, kc=kc):
                                def go(e, src):
                                    return e.dma_start(out=cache[strip, kc], in_=src)
                                return go
                            sf.append(mk())
                        tiles.append((w_in, s_in, cch, strip * 512, sf))
            for cch in range(DN_CC):
                for hs in range(HST):
                    sf = []
                    for lh in range(2):
                        r0 = (cch + 64 * lh) * 128

                        def mk(r0=r0, hs=hs):
                            def go(e, src):
                                return e.dma_start(
                                    out=wdC[r0:r0 + 128, hs * 512:(hs + 1) * 512],
                                    in_=src)
                            return go
                        sf.append(mk())
                    tiles.append((wdB, sdE, cch, hs * 512, sf))
            return tiles

        def dq_make_feeder():
            dq_state = {"pend": [], "sv_after": {}, "st_cum": {}}
            tiles = dq_tile_list()
            n = len(tiles)
            succ = {}
            cur = {"i": 0, "primed": False}

            def prime():
                if cur["primed"]:
                    return
                cur["primed"] = True
                dq_state["bar"] = [("qS", C("qS"))]
                succ[0] = dq_loads(tiles[0], 0, dq_state)
                succ[1] = dq_loads(tiles[1], 1, dq_state)

            def feed(k):
                prime()
                for _ in range(k):
                    ti = cur["i"]
                    if ti >= n:
                        return
                    dequant_tile(tiles[ti][4], ti, succ[ti + 1], dq_state)
                    if ti + 2 < n:
                        succ[ti + 2] = dq_loads(tiles[ti + 2], ti + 2, dq_state)
                    elif ti + 2 == n:
                        DMA("gpsimd", lambda e: e.dma_start(
                            out=lnScr[0:1, 16:24], in_=lnScr[0:1, 24:32]),
                            inc="qG")
                        DMA("scalar", lambda e: e.dma_start(
                            out=lnScr[0:1, 0:8], in_=lnScr[0:1, 8:16]),
                            inc="qA")
                        succ[n] = (C("qG"), C("qA"))
                    if (ti + 1) % DRB == 0:
                        dq_flush(dq_state)
                    cur["i"] = ti + 1

            def finish():
                while cur["i"] < n:
                    feed(DRB)
                dq_flush(dq_state)
            return feed, finish

        def phase_ffn(b):
            state.setdefault("gcnt", 0)
            state.setdefault("gsA", {})
            state.setdefault("gsV", {})
            dq_total = C("dqw16")
            wo_stores = state["wo_stores_%d" % b]
            gu_free = {0: C("sT"), 1: C("sT")}
            wd_free = [C("sT")] * 4
            TCr = 1 if "q0" in debug else TCH
            NQr = 1 if "q0" in debug else NQ
            def g_load(tc, q, s2):
                strip = q * 8 + s2 // 2
                half = s2 % 2
                gsl = s2 % 2
                for (cache, dst) in ((wgC, wgs), (wuC, wus)):
                    DMA("sync", (lambda cache=cache, dst=dst, strip=strip,
                                 half=half, gsl=gsl: lambda e: e.dma_start(
                        out=dst[:, gsl, :, :],
                        in_=cache[strip].rearrange("c p n -> p c n")[
                            :, :, half * 256:(half + 1) * 256]))(),
                        waits=[("dqw16", dq_total), ("sT", gu_free[gsl])],
                        inc="qW")
                return C("qW")

            pre_ld = {}
            for tc in range(TCr):
                for q in range(NQr):
                    dEv = C("sV")
                    for s2 in range(16):
                        gsl = s2 % 2
                        if (tc, q, s2) in pre_ld:
                            ld = pre_ld.pop((tc, q, s2))
                        else:
                            ld = g_load(tc, q, s2)
                        TGUARD(waits=[("qW", ld)])
                        for nsl in range(2):
                            nq_i = s2 * 2 + nsl
                            par = state["gcnt"] % 2
                            gb, ub = ("E", "F") if par == 0 else ("A", "B")
                            g2 = state["gcnt"] - 2
                            for (wbuf, bank) in ((wgs, gb), (wus, ub)):
                                for c in range(KC):
                                    wt = [("qW", ld)]
                                    if c == 0:
                                        if g2 in state["gsA"]:
                                            wt += [("sA", state["gsA"][g2]),
                                                   ("sV", max(state["gsV"][g2],
                                                              dEv))]
                                        else:
                                            wt += [("sA", C("sA")),
                                                   ("sV", C("sV"))]
                                    E("tensor", (lambda wbuf=wbuf, gsl=gsl, c=c,
                                                 nsl=nsl, tc=tc, bank=bank:
                                        lambda t: t.matmul(
                                            ps[bank][:],
                                            wbuf[:, gsl, c,
                                                 nsl * 128:(nsl + 1) * 128],
                                            xnT[:, c, tc * 512:(tc + 1) * 512],
                                            start=(c == 0), stop=(c == KC - 1)))(),
                                        waits=wt, inc="sT")
                            mmd = C("sT")
                            wg = [("sT", mmd)]
                            wg.append(("sV", state["gsV"][g2]) if g2 in
                                      state["gsV"] else ("sV", C("sV")))
                            E("scalar", (lambda par=par, gb=gb: lambda s:
                                s.activation(out=gg[:, par, :], in_=ps[gb][:],
                                             func=AF.Gelu_apprx_tanh))(),
                                waits=wg, inc="sA", drain=True)
                            state["gsA"][state["gcnt"]] = C("sA")
                            E("vector", (lambda nq_i=nq_i, par=par, ub=ub:
                                lambda v: v.tensor_tensor(
                                    out=intq[:, nq_i, :], in0=ps[ub][:],
                                    in1=gg[:, par, :], op=OP.mult))(),
                                waits=[("sA", state["gsA"][state["gcnt"]]),
                                       ("sT", mmd)], inc="sV",
                                drain=(nq_i == 31))
                            state["gsV"][state["gcnt"]] = C("sV")
                            state["gcnt"] += 1
                        gu_free[gsl] = C("sT")
                        if ("ffn0" in debug and b == 0 and tc == 0 and q == 0
                                and s2 == 0):
                            dump("wgs", wgs[:, 0, :, :])
                            dump("gg", gg[:, 0, :])
                    g_done = C("sV")
                    if "ffn0" in debug and b == 0 and tc == 0 and q == 0:
                        dump("intq", intq[:])
                    ids = [(hp, ch) for hp in range(HST) for ch in range(NCHQ)]

                    def issue_wds(j):
                        hp, ch = ids[j]
                        npr = q * NCHQ + ch
                        dsl = j % 4
                        DMA("sync", (lambda npr=npr, hp=hp, dsl=dsl: lambda e:
                            e.dma_start(
                                out=wds[:, dsl, :],
                                in_=wdC[npr * 128:(npr + 1) * 128,
                                        hp * 512:(hp + 1) * 512]))(),
                            waits=[("sT", wd_free[dsl])], inc="qS")
                        return C("qS")

                    ld_after = [None] * (len(ids) + 1)
                    ld_after[0] = issue_wds(0)
                    for j, (hp, ch) in enumerate(ids):
                        if j + 1 < len(ids):
                            ld_after[j + 1] = issue_wds(j + 1)
                        else:
                            DMA("sync", lambda e: e.dma_start(
                                out=lnScr[0:1, 0:8], in_=lnScr[0:1, 8:16]),
                                inc="qS")
                            ld_after[j + 1] = C("qS")
                        dsl = j % 4
                        E("gpsimd", lambda g: g.nop(cycle_cnt=1024),
                          waits=[("qS", ld_after[j + 1])], inc="qE")
                        for tt4 in range(4):
                            wt = [("qE", C("qE")), ("sV", g_done)]
                            if ch == 0:
                                wt += [("sV", C("sV")), ("sA", C("sA"))]
                            E("tensor", (lambda ch=ch, tt4=tt4, dsl=dsl,
                                         bank="ABCD"[tt4]: lambda t: t.matmul(
                                ps[bank][:],
                                intq[:, ch, tt4 * 128:(tt4 + 1) * 128],
                                wds[:, dsl, :],
                                start=(ch == 0), stop=(ch == NCHQ - 1)))(),
                                waits=wt, inc="sT")
                        wd_free[dsl] = C("sT")
                        if ch == NCHQ - 1:
                            dn = C("sT")
                            for tt4 in range(4):
                                bank = "ABCD"[tt4]
                                if q == 0:
                                    E("vector", (lambda tt4=tt4, hp=hp, bank=bank:
                                        lambda v: v.tensor_copy(
                                            ffa[:, tt4, hp * 512:(hp + 1) * 512],
                                            ps[bank][:]))(),
                                        waits=[("sT", dn - 3 + tt4),
                                               ("qG", wo_stores)],
                                        inc="sV")
                                else:
                                    E("vector", (lambda tt4=tt4, hp=hp, bank=bank:
                                        lambda v: v.tensor_tensor(
                                            out=ffa[:, tt4, hp * 512:(hp + 1) * 512],
                                            in0=ps[bank][:],
                                            in1=ffa[:, tt4, hp * 512:(hp + 1) * 512],
                                            op=OP.add))(),
                                        waits=[("sT", dn - 3 + tt4)], inc="sV")
                for tt4 in range(4 if "q0" not in debug else 0):
                    tt = tc * 4 + tt4
                    DMA("gpsimd", (lambda b=b, tt=tt: lambda g: g.dma_start(
                        out=xfb[:], in_=x_d[b, tt * 128:(tt + 1) * 128, :]))(),
                        waits=[("sV", C("sV")), ("qG", wo_stores),
                               ("sT", C("sT"))], inc="qG")
                    DMA("gpsimd", (lambda b=b, tt=tt: lambda g: g.dma_start(
                        out=afb[:], in_=atD[b, tt * 128:(tt + 1) * 128, :]))(),
                        inc="qG")
                    DMA("gpsimd", lambda g: g.dma_start(
                        out=lnScr[0:1, 48:56], in_=lnScr[0:1, 56:64]), inc="qG")
                    ld = C("qG")
                    E("vector", lambda v: v.tensor_tensor(
                        out=ob[:], in0=xfb[:], in1=afb[:], op=OP.add),
                        waits=[("qG", ld), ("oW", C("oW"))], inc="sV")
                    E("vector", (lambda tt4=tt4: lambda v: v.tensor_tensor(
                        out=ob[:], in0=ffa[:, tt4, :], in1=ob[:], op=OP.add))(),
                        inc="sV", drain=True)
                    DMA("gpsimd", (lambda b=b, tt=tt: lambda g: g.dma_start(
                        out=out_d[b, tt * 128:(tt + 1) * 128, :], in_=ob[:]))(),
                        waits=[("sV", C("sV"))], inc="oW")

        def dump(nm, src):
            DMA("gpsimd", (lambda nm=nm, src=src: lambda g: g.dma_start(
                out=dbg[nm], in_=src))(),
                waits=[("sV", C("sV")), ("sA", C("sA")), ("sT", C("sT")),
                       ("qS", C("qS")), ("qG", C("qG")), ("qA", C("qA")),
                       ("dqw16", C("dqw16"))],
                inc="oW")

        import os as _os
        FEEDMASK = set((_os.environ.get("KFEED") or "qkv,norms,pass2,wo").split(","))
        nheads = H if "att" in phases else (1 if "att1" in phases else 0)
        state["dq_feed"] = None
        if "deq" in phases:
            fd, fn = dq_make_feeder()
            state["dq_feed"] = fd
            state["dq_finish"] = fn
        for b in range(1 if "b0" in phases else BC):
            if "ln" in phases:
                sch.phase = "ln%d" % b
                phase_ln(b)
                if "lnstats" in debug and b == 0:
                    dump("lnT", lnT[:])
                    dump("lnR", lnR[:])
                    dump("lnS", lnS[:])
                    dump("lnQ", lnQ[:])
                if "ln" in debug and b == 0:
                    dump("xnT", xnT[:])
                    dump("xnt", xnt_t[:])
                    dump("lnT", lnT[:])
                    dump("sdW", sdW[:])
                    dump("lnR", lnR[:])
                    dump("lnS", lnS[:])
                    dump("lnQ", lnQ[:])
            if nheads:
                sch.phase = "att%d" % b
                load_head_w(b, 0, 0)
            for h in range(nheads):
                if h + 1 < nheads:
                    load_head_w(b, h + 1, (h + 1) % 2)
                phase_qkv(b, h, h % 2, ((wq_h, qT), (wk_h, kT)))
                phase_qkv(b, h, h % 2, ((wv_h, vTt),))
                phase_norms(b, h)
                phase_vtok(b, h)
                if "qkv" in debug and b == 0 and h == 0:
                    dump("qT", qT[:])
                    dump("kT", kT[:])
                    dump("vtok", vtok[:])
                phase_pass1(b, h)
                if "p1" in debug and b == 0 and h == 0:
                    dump("rrow", rrow[:])
                    dump("negc", negc[:])
                phase_crow(b, h)
                phase_pass2_pv(b, h)
                if "p2" in debug and b == 0 and h == 0:
                    dump("ptb", ptb[:])
            if "wo" in phases:
                sch.phase = "wo%d" % b
                phase_wo(b)
                if "wo" in debug and b == 0:
                    DMA("gpsimd", lambda g: g.dma_start(out=dbg["att"], in_=atD[0]),
                        waits=[("qG", C("qG"))], inc="oW")
            if b == 0 and "deq" in phases:
                sch.phase = "deq"
                state["dq_finish"]()
                state["dq_feed"] = None
                if "deq" in debug:
                    DMA("gpsimd", lambda g: g.dma_start(out=dbg["wg0"], in_=wgC[0]),
                        waits=[("dqw16", C("dqw16"))], inc="oW")
                    DMA("gpsimd", lambda g: g.dma_start(out=dbg["wd0"],
                                                        in_=wdC[0:128, :]),
                        inc="oW")
                    DMA("gpsimd", lambda g: g.dma_start(out=dbg["wg31"],
                                                        in_=wgC[31]), inc="oW")
                    DMA("gpsimd", lambda g: g.dma_start(out=dbg["wu31"],
                                                        in_=wuC[31]), inc="oW")
                    DMA("gpsimd", lambda g: g.dma_start(
                        out=dbg["wd63"][0:128, :], in_=wdC[8064:8192, :]),
                        inc="oW")
                    DMA("gpsimd", lambda g: g.dma_start(
                        out=dbg["wd63"][128:256, :], in_=wdC[16256:16384, :]),
                        inc="oW")
            if "ffn" in phases:
                sch.phase = "ffn%d" % b
                phase_ffn(b)
                if "ffn" in debug and b == 0:
                    dump("intq", intq[:])
                    dump("ffa", ffa[:])

        E("gpsimd", lambda g: g.nop(), waits=[("oW", sch.count("oW")),
           ("qG", C("qG")), ("qA", C("qA")), ("qS", C("qS")),
           ("dqw16", C("dqw16"))])

        block = ctx(nc.Block())

        SPC = {"vector": 1024, "scalar": 1024, "gpsimd": 256}
        import os
        SIMMODE = bool(os.environ.get("KSIM"))

        PROF = bool(os.environ.get("KPROF"))

        def runner(name):
            def go(eng):
                seen = {}
                cur = [None, None]
                for (fn, waits, inc, incn, drn, pre, pad, ph) in sch.ops[name]:
                    if PROF and ph != cur[0]:
                        if cur[0] is not None:
                            nc.leave_named_scope(cur[0], cur[1], False)
                        cur[0] = ph
                        cur[1] = nc.enter_named_scope(ph, False)[0]
                    for (s, v) in waits:
                        if seen.get(s, -1) < v:
                            eng.wait_ge(sems[s], v)
                            seen[s] = v
                    if pre and not SIMMODE:
                        eng.nop(cycle_cnt=SPC.get(name, 256))
                    inst = fn(eng)
                    if drn:
                        if SIMMODE:
                            sp = eng.drain(fusable=False)
                        else:
                            eng.drain(fusable=False)
                            sp = eng.nop(cycle_cnt=SPC.get(name, 256))
                        if inc is not None:
                            sp.then_inc(sems[inc], incn)
                    elif inc is not None:
                        inst.then_inc(sems[inc], incn)
                    if pad and not SIMMODE:
                        eng.nop(cycle_cnt=64)
                if PROF and cur[0] is not None:
                    nc.leave_named_scope(cur[0], cur[1], False)
            return go

        block.sync(runner("sync"))
        block.scalar(runner("scalar"))
        block.vector(runner("vector"))
        block.tensor(runner("tensor"))
        block.gpsimd(runner("gpsimd"))

    return nc


def _host_prep(inputs):
    x = np.ascontiguousarray(np.asarray(inputs["L0_attn_x"], dtype=np.float32))
    dperm = np.concatenate([np.arange(0, D, 2), np.arange(1, D, 2)])
    nperm = np.concatenate([np.arange(0, NF, 2), np.arange(1, NF, 2)])

    def wT(w):
        return np.ascontiguousarray(np.asarray(w, dtype=np.float32).T[dperm])

    wq = wT(inputs["L0_attn_Wq"])
    wk = wT(inputs["L0_attn_Wk"])
    wv = wT(inputs["L0_attn_Wv"])
    wo = np.ascontiguousarray(np.asarray(inputs["L0_attn_Wo"], dtype=np.float32).T)

    def bytes_of(p):
        a = np.ascontiguousarray(np.asarray(p, dtype=np.int32))
        return a.view(np.uint8).reshape(a.shape[0], a.shape[1], 4)[:, :, 0]

    wgB = np.ascontiguousarray(bytes_of(inputs["L0_ffn_gate_W_packed"]).T[:, nperm])
    wuB = np.ascontiguousarray(bytes_of(inputs["L0_ffn_up_W_packed"]).T[:, nperm])
    wdB = np.ascontiguousarray(bytes_of(inputs["L0_ffn_down_W_packed"]).T)
    sgE = np.ascontiguousarray(
        np.asarray(inputs["L0_ffn_gate_scales"], dtype=np.float32).T[:, nperm])
    suE = np.ascontiguousarray(
        np.asarray(inputs["L0_ffn_up_scales"], dtype=np.float32).T[:, nperm])
    sdE = np.ascontiguousarray(
        np.asarray(inputs["L0_ffn_down_scales"], dtype=np.float32).T)
    idf = np.eye(128, dtype=np.float32)
    shared = dict(wqT=wq, wkT=wk, wvT=wv, woT=wo, wgB=wgB, sgE=sgE,
                  wuB=wuB, suE=suE, wdB=wdB, sdE=sdE, idf=idf)
    return x, shared


def kernel(**inputs):
    x, shared = _host_prep(inputs)
    nc = build_nc()
    in_maps = []
    for c in range(NCORES):
        m = dict(shared)
        m["x"] = np.ascontiguousarray(x[c * BC:(c + 1) * BC])
        in_maps.append(m)
    res = run_bass_kernel_spmd(nc, in_maps, core_ids=list(range(NCORES)))
    out = np.concatenate([r["out"] for r in res.results], axis=0)
    return out
